# version 03
# baseline (speedup 1.0000x reference)
import numpy as np
import concourse.bass as bass
import concourse.mybir as mybir
from concourse.bass_utils import run_bass_kernel_spmd

F32 = mybir.dt.float32
BF16 = mybir.dt.bfloat16
AF = mybir.ActivationFunctionType
OP = mybir.AluOpType
AX = mybir.AxisListType

D = 2048
TOK = 1024
G = 2
GT = TOK // G
CH = GT // 128
NC = 8

PHI4_32 = float(2 ** 20 + 1)
PHI4_64 = float(2 ** 49 + 1)
K358 = 448.0 * 0.8
EPS_LN = 1e-5
EPS_ATTN = 1e-9
CSC = 128.0 ** -0.5


class Sched:
    ENG2BLOCK = {"pe": "tensor", "dve": "vector", "act": "scalar",
                 "pool": "gpsimd", "sync": "sync"}

    def __init__(self):
        self.q = {e: [] for e in self.ENG2BLOCK}
        self.cnt = {"pe": 0, "dve": 0, "act": 0, "pool": 0,
                    "dsync": 0, "dpool": 0, "cc": 0}
        self.gate = []

    def op(self, eng, fn, deps=(), sig=False):
        deps = [d for d in deps if d is not None] + self.gate
        ev = None
        if sig:
            self.cnt[eng] += 1
            ev = (eng, self.cnt[eng])
        self.q[eng].append([deps, fn, ("c", ev) if ev else None])
        return ev

    def dma(self, eng, out, in_, deps=()):
        sem = "dsync" if eng == "sync" else "dpool"
        deps = [d for d in deps if d is not None] + self.gate
        self.cnt[sem] += 16
        ev = (sem, self.cnt[sem])
        self.q[eng].append(
            [deps, (lambda E, o=out, i=in_: E.dma_start(out=o, in_=i)),
             ("d", ev)])
        return ev

    def cc(self, fn, deps=()):
        deps = [d for d in deps if d is not None]
        self.cnt["cc"] += 1
        ev = ("cc", self.cnt["cc"])
        self.q["pool"].append([deps, fn, ("c", ev)])
        return ev

    def barrier(self, engines=("pe", "dve", "act", "pool")):
        evs = []
        for e in engines:
            if not self.q[e]:
                continue
            last = self.q[e][-1]
            if last[2] is None:
                self.cnt[e] += 1
                ev = (e, self.cnt[e])
                last[2] = ("c", ev)
            else:
                ev = last[2][1]
            evs.append(ev)
        return evs

    def emit(self, block, semh):
        for eng, bname in self.ENG2BLOCK.items():
            items = self.q[eng]
            own = eng

            def body(E, items=items, own=own):
                waited = {}
                for deps, fn, sig in items:
                    for (sname, v) in deps:
                        if waited.get(sname, 0) < v:
                            E.wait_ge(semh[sname], v)
                            waited[sname] = v
                    ins = fn(E)
                    if sig is not None:
                        kind, (sname, v) = sig
                        ins.then_inc(semh[sname], 16 if kind == "d" else 1)

            getattr(block, bname)(body)


DBG = set()
FENCES = set()


def build_program():
    nc = bass.Bass("TRN2", target_bir_lowering=False,
                   dynamic_dma_scratch_size=8192)
    dbg_outs = {}

    def dbg_decl(name, shape, dt=F32):
        dbg_outs[name] = nc.dram_tensor("dbg_" + name, shape, dt,
                                        kind="ExternalOutput")
        return dbg_outs[name]

    x_d = nc.dram_tensor("x", [TOK, D], F32, kind="ExternalInput")
    wqkv_d = nc.dram_tensor("wqkvT", [D, 3 * D], F32, kind="ExternalInput")
    wout_d = nc.dram_tensor("woutT", [D, D], F32, kind="ExternalInput")
    wffn_d = nc.dram_tensor("wffnT", [D, D], F32, kind="ExternalInput")
    bias_d = nc.dram_tensor("biasr", [1, D], F32, kind="ExternalInput")
    tri_d = nc.dram_tensor("tri", [128, 128], F32, kind="ExternalInput")
    id_d = nc.dram_tensor("ident", [128, 128], F32, kind="ExternalInput")
    ones1_d = nc.dram_tensor("ones1", [1, 128], F32, kind="ExternalInput")
    onesc_d = nc.dram_tensor("onesc", [128, 1], F32, kind="ExternalInput")
    flago_d = nc.dram_tensor("flago", [1, 1], F32, kind="ExternalInput")
    flagor_d = nc.dram_tensor("flagor", [1, 128], F32, kind="ExternalInput")
    flage_d = nc.dram_tensor("flage", [1, 1], F32, kind="ExternalInput")
    y_d = nc.dram_tensor("y", [TOK, D], F32, kind="ExternalOutput")
    xmid_d = nc.dram_tensor("xmid_i", [TOK, D], F32)
    spq_d = nc.dram_tensor("spq_i", [128, CH * D], F32)
    spk_d = nc.dram_tensor("spk_i", [128, CH * D], F32)
    spkv_d = nc.dram_tensor("spkv_i", [128, CH * D], F32)
    spi_d = nc.dram_tensor("spi_i", [1, 2 * D], F32)
    spo_d = nc.dram_tensor("spo_i", [1, 2 * D], F32)
    cxi_d = nc.dram_tensor("cxi_i", [1, 16], F32)
    cxo_d = nc.dram_tensor("cxo_i", [1, 16], F32, addr_space="Shared")

    from contextlib import ExitStack
    s = Sched()
    with ExitStack() as es:
        e = es.enter_context
        PHIQ = e(nc.sbuf_tensor("PHIQ", [128, CH * D], F32))
        KV = e(nc.sbuf_tensor("KV", [128, CH * D], F32))
        PHIK = e(nc.sbuf_tensor("PHIK", [128, CH * D], F32))
        WB = [e(nc.sbuf_tensor(f"WB{i}", [128, 2048], F32)) for i in range(2)]
        QW = e(nc.sbuf_tensor("QW", [128, 2048], BF16))
        QSCR = e(nc.sbuf_tensor("QSCR", [128, 4096], F32))
        YB = e(nc.sbuf_tensor("YB", [128, D], F32))
        TV1 = e(nc.sbuf_tensor("TV1", [1, 2 * D], F32))
        CARKV = e(nc.sbuf_tensor("CARKV", [1, D], F32))
        CARK = e(nc.sbuf_tensor("CARK", [1, D], F32))
        BIASR = e(nc.sbuf_tensor("BIASR", [1, D], F32))
        TRI = e(nc.sbuf_tensor("TRI", [128, 128], F32))
        IDN = e(nc.sbuf_tensor("IDN", [128, 128], F32))
        ONES1 = e(nc.sbuf_tensor("ONES1", [1, 128], F32))
        ONESC = e(nc.sbuf_tensor("ONESC", [128, 1], F32))
        FLAGO = e(nc.sbuf_tensor("FLAGO", [1, 1], F32))
        FLAGOR = e(nc.sbuf_tensor("FLAGOR", [1, 128], F32))
        FLAGE = e(nc.sbuf_tensor("FLAGE", [1, 1], F32))
        CX = e(nc.sbuf_tensor("CX", [1, 16], F32))
        CX2 = e(nc.sbuf_tensor("CX2", [1, 16], F32))
        SC = e(nc.sbuf_tensor("SC", [1, 16], F32))
        ST = e(nc.sbuf_tensor("ST", [128, 8], F32))
        M1 = e(nc.sbuf_tensor("M1", [128, CH], F32))
        I1 = e(nc.sbuf_tensor("I1", [128, CH], F32))
        M2 = e(nc.sbuf_tensor("M2", [128, 2 * CH], F32))
        I2 = e(nc.sbuf_tensor("I2", [128, 2 * CH], F32))
        AMX = e(nc.sbuf_tensor("AMX", [128, 8], F32))
        AMW = e(nc.sbuf_tensor("AMW", [128, 16], F32))
        B128 = e(nc.sbuf_tensor("B128", [128, 4], F32))
        PB = [e(nc.psum_tensor(f"PB{i}", [128, 512], F32)) for i in range(8)]
        semh = {n: e(nc.semaphore(n)) for n in
                ("pe", "dve", "act", "pool", "dsync", "dpool", "cc")}

        SL0 = QSCR[:, 0:2048]
        SL1 = QSCR[:, 2048:4096]

        bank_free = [[] for _ in range(8)]
        wb_free = [[], []]
        yb_free = [[]]

        def fence(name):
            if name != "rl" and "ALL" not in FENCES and name not in FENCES:
                return
            evs = s.barrier(("pe", "dve", "act", "pool"))
            evs += [("dsync", s.cnt["dsync"]), ("dpool", s.cnt["dpool"])]
            if s.cnt["cc"]:
                evs.append(("cc", s.cnt["cc"]))
            s.gate = evs

        def dump(name, src_ap, shape, dt=F32):
            if name not in DBG:
                return
            dd = dbg_decl(name, shape, dt)
            ev = s.dma("sync", dd[tuple(slice(0, x) for x in shape)], src_ap,
                       deps=s.barrier(("pe", "dve", "act", "pool")))
            s.gate = [ev]

        c_tri = s.dma("sync", TRI[:], tri_d[:, :])
        c_id = s.dma("sync", IDN[:], id_d[:, :])
        c_o1 = s.dma("sync", ONES1[:], ones1_d[:, :])
        c_oc = s.dma("sync", ONESC[:], onesc_d[:, :])
        c_br = s.dma("sync", BIASR[:], bias_d[:, :])
        c_fo = s.dma("sync", FLAGO[:], flago_d[:, :])
        c_for = s.dma("sync", FLAGOR[:], flagor_d[:, :])
        c_fe = s.dma("sync", FLAGE[:], flage_d[:, :])

        iv0 = s.op("dve", lambda E: E.memset(B128[:, 0:1], 1.0), sig=True)
        iv1 = s.op("dve", lambda E: E.tensor_scalar(
            out=B128[:, 0:1], in0=B128[:, 0:1], scalar1=float(2.0 ** -30),
            scalar2=None, op0=OP.add), deps=[iv0], sig=True)
        ind_ev = s.op("dve", lambda E: E.tensor_scalar(
            out=B128[:, 0:1], in0=B128[:, 0:1], scalar1=1.0,
            scalar2=float(2.0 ** 30), op0=OP.subtract, op1=OP.mult),
            deps=[iv1], sig=True)

        amw_parts = []
        for i in range(16):
            src = bass.AP(wffn_d, i * 128, [[D, 128], [128 * D, 16], [1, 128]])
            dst = WB[i % 2][:].rearrange("p (k c) -> p k c", k=16)
            ld = s.dma("sync", dst, src, deps=wb_free[i % 2])
            rd = s.op("dve", lambda E, i=i, b=i % 2: E.tensor_reduce(
                out=AMW[:, i:i + 1], in_=WB[b][:], axis=AX.X, op=OP.max,
                apply_absolute_value=True), deps=[ld], sig=True)
            wb_free[i % 2] = [rd]
            amw_parts.append(rd)
        amw_ev = s.op("dve", lambda E: E.tensor_reduce(
            out=AMW[:, 0:1], in_=AMW[:, 0:16], axis=AX.X, op=OP.max,
            apply_absolute_value=True), deps=amw_parts, sig=True)

        def ln_stats(chunk_src, mean_ap, inv_ap, deps):
            r1 = s.op("dve", lambda E: E.tensor_reduce(
                out=ST[:, 0:1], in_=chunk_src, axis=AX.X, op=OP.add),
                deps=deps, sig=True)
            sq = s.op("act", lambda E: E.activation(
                out=SL1, in_=chunk_src, func=AF.Square), deps=deps, sig=True)
            r2 = s.op("dve", lambda E: E.tensor_reduce(
                out=ST[:, 1:2], in_=SL1, axis=AX.X, op=OP.add),
                deps=[sq], sig=True)
            g1 = s.op("dve", lambda E: E.tensor_scalar(
                out=mean_ap, in0=ST[:, 0:1], scalar1=1.0 / D, scalar2=None,
                op0=OP.mult), deps=[r1], sig=True)
            g2 = s.op("dve", lambda E: E.tensor_scalar(
                out=ST[:, 2:3], in0=ST[:, 1:2], scalar1=1.0 / D, scalar2=None,
                op0=OP.mult), deps=[r2], sig=True)
            g3 = s.op("dve", lambda E: E.tensor_tensor(
                out=ST[:, 3:4], in0=mean_ap, in1=mean_ap, op=OP.mult),
                deps=[g1], sig=True)
            g4 = s.op("dve", lambda E: E.tensor_tensor(
                out=ST[:, 4:5], in0=ST[:, 2:3], in1=ST[:, 3:4],
                op=OP.subtract), deps=[g2, g3], sig=True)
            ve = s.op("dve", lambda E: E.tensor_scalar(
                out=ST[:, 5:6], in0=ST[:, 4:5], scalar1=EPS_LN, scalar2=None,
                op0=OP.add), deps=[g4], sig=True)
            sd = s.op("act", lambda E: E.activation(
                out=ST[:, 6:7], in_=ST[:, 5:6], func=AF.Sqrt), deps=[ve],
                sig=True)
            return s.op("dve", lambda E: E.reciprocal(
                out=inv_ap, in_=ST[:, 6:7]), deps=[sd], sig=True)

        def transpose_to(src_tile_fn, dst_fn, deps):
            evs = []
            for jb in range(4):
                bidx = 6 + (jb % 2)
                bank = PB[bidx]
                last = None
                for t in range(4):
                    j = 4 * jb + t
                    dd = (deps + bank_free[bidx] + [c_id]) if t == 0 else []
                    last = s.op("pe", lambda E, j=j, t=t, b=bank: E.transpose(
                        b[:, 128 * t:128 * t + 128], src_tile_fn(j), IDN[:]),
                        deps=dd, sig=(t == 3))
                ev = s.op("act", lambda E, jb=jb, b=bank: E.copy(
                    out=dst_fn(jb), in_=b[:, 0:512]), deps=[last], sig=True)
                bank_free[bidx] = [ev]
                evs.append(ev)
            return evs

        def phase_A(g, XT):
            tp_evs = []
            for mg in range(CH):
                row0 = g * GT + mg * 128
                ld = s.dma("sync", YB[:], x_d[row0:row0 + 128, :],
                           deps=yb_free[0])
                iv = ln_stats(YB[:], M1[:, mg:mg + 1], I1[:, mg:mg + 1], [ld])
                ap = s.op("dve", lambda E, mg=mg: E.tensor_scalar(
                    out=SL0, in0=YB[:], scalar1=M1[:, mg:mg + 1],
                    scalar2=I1[:, mg:mg + 1], op0=OP.subtract, op1=OP.mult),
                    deps=[iv] + tp_evs[-4:], sig=True)
                yb_free[0] = [ap]
                tp_evs = transpose_to(
                    lambda j: SL0[:, 128 * j:128 * j + 128],
                    lambda jb, mg=mg: bass.AP(
                        XT, 2048 * jb + 128 * mg,
                        [[CH * D, 128], [512, 4], [1, 128]]),
                    [ap])
            return tp_evs

        def phase_B(g, XT, xt_ready, totg_fn, extra):
            for fs in range(4):
                phk_evs = [None] * CH
                kv_evs = [None] * CH
                for tgt in range(3):
                    col0 = tgt * D + fs * 512
                    mm_last = [None] * CH
                    wave = 0
                    for half in range(2):
                        for kcr in range(2):
                            wb = WB[wave % 2]
                            src = bass.AP(
                                wqkv_d,
                                col0 + 256 * half + kcr * 8 * 128 * 3 * D,
                                [[3 * D, 128], [128 * 3 * D, 8], [1, 256]])
                            dd = s.dma("sync",
                                       wb[:].rearrange("p (k c) -> p k c", k=8),
                                       src, deps=wb_free[wave % 2])
                            for kc8 in range(8):
                                kc = kcr * 8 + kc8
                                for mg in range(CH):
                                    st = (half == 0 and kc == 0)
                                    deps = [dd]
                                    if st:
                                        deps += bank_free[mg] + [xt_ready]
                                    mm_last[mg] = s.op(
                                        "pe",
                                        lambda E, mg=mg, kc=kc, kc8=kc8,
                                        h=half, wb=wb, XT=XT: E.matmul(
                                            PB[mg][:, 256 * h:256 * h + 256],
                                            XT[:, 512 * kc + 128 * mg:
                                               512 * kc + 128 * mg + 128],
                                            wb[:, 256 * kc8:256 * kc8 + 256],
                                            start=(h == 0 and kc == 0),
                                            stop=(h == 1 and kc == 15)),
                                        deps=deps,
                                        sig=((kc8 == 7 and mg == CH - 1)
                                             or (half == 1 and kc == 15)))
                            wb_free[wave % 2] = [mm_last[CH - 1]]
                            wave += 1
                    for mg in range(CH):
                        bank = PB[mg]
                        dsl = slice(D * mg + 512 * fs, D * mg + 512 * fs + 512)
                        if tgt < 2:
                            out = PHIQ if tgt == 0 else PHIK
                            t1 = s.op("dve", lambda E, bank=bank: E.tensor_scalar(
                                out=SL0[:, 0:512], in0=bank[:], scalar1=0.0,
                                scalar2=None, op0=OP.min),
                                deps=[mm_last[mg]], sig=True)
                            ex = s.op("act", lambda E: E.activation(
                                out=SL0[:, 512:1024], in_=SL0[:, 0:512],
                                func=AF.Exp), deps=[t1], sig=True)
                            rl = s.op("act", lambda E, bank=bank: E.activation(
                                out=SL0[:, 1024:1536], in_=bank[:],
                                func=AF.Relu), deps=[mm_last[mg]], sig=True)
                            ph = s.op("dve",
                                      lambda E, out=out, dsl=dsl: E.tensor_tensor(
                                          out=out[:, dsl], in0=SL0[:, 512:1024],
                                          in1=SL0[:, 1024:1536], op=OP.add),
                                      deps=[ex, rl] + extra, sig=True)
                            bank_free[mg] = [t1, rl]
                            if tgt == 1:
                                phk_evs[mg] = ph
                        else:
                            kve = s.op("dve",
                                       lambda E, dsl=dsl, bank=bank:
                                       E.tensor_tensor(
                                           out=KV[:, dsl], in0=PHIK[:, dsl],
                                           in1=bank[:], op=OP.mult),
                                       deps=[mm_last[mg], phk_evs[mg]] + extra,
                                       sig=True)
                            bank_free[mg] = [kve]
                            kv_evs[mg] = kve
                for (tens, base, sevs) in ((KV, 0, kv_evs), (PHIK, D, phk_evs)):
                    mms = []
                    for mg in range(CH):
                        mms.append(s.op(
                            "pe", lambda E, tens=tens, mg=mg, fs=fs: E.matmul(
                                PB[4][0:1, 0:512], ONESC[:],
                                tens[:, D * mg + 512 * fs:
                                     D * mg + 512 * fs + 512],
                                start=(mg == 0), stop=(mg == CH - 1)),
                            deps=(sevs + bank_free[4] + [c_oc])
                            if mg == 0 else [], sig=(mg == CH - 1)))
                    tev = s.op("act", lambda E, base=base, fs=fs,
                               totg_fn=totg_fn: E.copy(
                                   out=totg_fn(base, fs), in_=PB[4][0:1, 0:512]),
                               deps=[mms[-1]], sig=True)
                    bank_free[4] = [tev]
            return s.barrier(("dve", "act", "pe"))

        def phase_C(g, src_deps, car_init, car0):
            r_evs = []
            car_ev = {}
            for fs in range(4):
                car_ev[("kv", fs)] = list(car_init)
                car_ev[("k", fs)] = list(car_init)
            for mg in range(CH):
                for fs in range(4):
                    sl = slice(D * mg + 512 * fs, D * mg + 512 * fs + 512)
                    evd = {}
                    for nm, tens, car, bi in (("kv", KV, CARKV, 4),
                                              ("k", PHIK, CARK, 5)):
                        bc = bi + 2
                        s.op("pe", lambda E, tens=tens, sl=sl, bi=bi:
                             E.matmul(PB[bi][:], TRI[:], tens[:, sl],
                                      start=True, stop=False),
                             deps=bank_free[bi] + src_deps + [c_tri])
                        if mg == 0:
                            terms = [(row, scl, rfn(nm, fs))
                                     for (row, scl, rfn) in car0]
                        else:
                            terms = [(ONES1[:], ONES1[0:1, 0:1],
                                      car[0:1, 512 * fs:512 * fs + 512])]
                        t2 = t3 = None
                        nt = len(terms)
                        for ti, (row, scl, rhs) in enumerate(terms):
                            t2 = s.op("pe", lambda E, row=row, rhs=rhs, bi=bi,
                                      last=(ti == nt - 1):
                                      E.matmul(PB[bi][:], row, rhs,
                                               start=False, stop=last),
                                      deps=car_ev[(nm, fs)] + [c_o1, c_for],
                                      sig=(ti == nt - 1))
                        s.op("pe", lambda E, tens=tens, sl=sl, bc=bc:
                             E.matmul(PB[bc][0:1, 0:512], ONESC[:],
                                      tens[:, sl], start=True, stop=False),
                             deps=bank_free[bc] + [c_oc])
                        for ti, (row, scl, rhs) in enumerate(terms):
                            t3 = s.op("pe", lambda E, scl=scl, rhs=rhs, bc=bc,
                                      last=(ti == nt - 1):
                                      E.matmul(PB[bc][0:1, 0:512], scl, rhs,
                                               start=False, stop=last),
                                      deps=car_ev[(nm, fs)],
                                      sig=(ti == nt - 1))
                        cx = s.op("act", lambda E, car=car, fs=fs, bc=bc:
                                  E.copy(out=car[0:1, 512 * fs:512 * fs + 512],
                                         in_=PB[bc][0:1, 0:512]),
                                  deps=[t3, t2], sig=True)
                        bank_free[bc] = [cx]
                        car_ev[(nm, fs)] = [cx]
                        evd[nm] = t2
                        evd[nm + "x"] = cx
                    num = s.op("dve", lambda E, sl=sl: E.tensor_tensor(
                        out=SL0[:, 0:512], in0=PHIQ[:, sl], in1=PB[4][:],
                        op=OP.mult), deps=[evd["kv"]], sig=True)
                    den = s.op("dve", lambda E, sl=sl: E.tensor_tensor(
                        out=SL0[:, 512:1024], in0=PHIQ[:, sl], in1=PB[5][:],
                        op=OP.mult), deps=[evd["k"]], sig=True)
                    bank_free[4] = [num, evd["kvx"]]
                    bank_free[5] = [den, evd["kx"]]
                    de = s.op("pool", lambda E: E.tensor_scalar(
                        out=SL0[:, 512:1024], in0=SL0[:, 512:1024],
                        scalar1=EPS_ATTN, scalar2=None, op0=OP.add),
                        deps=[den], sig=True)
                    rc = s.op("dve", lambda E: E.reciprocal(
                        out=SL0[:, 1024:1536], in_=SL0[:, 512:1024]),
                        deps=[de], sig=True)
                    rv = s.op("dve", lambda E, sl=sl: E.tensor_tensor(
                        out=PHIQ[:, sl], in0=SL0[:, 0:512],
                        in1=SL0[:, 1024:1536], op=OP.mult), deps=[rc], sig=True)
                    r_evs.append(rv)
            return r_evs

        def phase_D(g, RT, r_deps):
            tp = []
            for mg in range(CH):
                tp += transpose_to(
                    lambda j, mg=mg: PHIQ[:, D * mg + 128 * j:
                                          D * mg + 128 * j + 128],
                    lambda jb, mg=mg: bass.AP(
                        RT, 2048 * jb + 128 * mg,
                        [[CH * D, 128], [512, 4], [1, 128]]),
                    r_deps)
            rt_ready = s.barrier(("act",))
            st_evs = []
            for os in range(4):
                xld = [None] * CH
                for mg in range(CH):
                    row0 = g * GT + mg * 128
                    xld[mg] = s.dma(
                        "sync", SL1[:, 512 * mg:512 * mg + 512],
                        x_d[row0:row0 + 128, 512 * os:512 * os + 512],
                        deps=sl1_free[0])
                mm_last = [None] * CH
                wave = 0
                for half in range(2):
                    for kcr in range(2):
                        wb = WB[wave % 2]
                        src = bass.AP(
                            wout_d, os * 512 + 256 * half + kcr * 8 * 128 * D,
                            [[D, 128], [128 * D, 8], [1, 256]])
                        dd = s.dma("sync",
                                   wb[:].rearrange("p (k c) -> p k c", k=8),
                                   src, deps=wb_free[wave % 2])
                        for kc8 in range(8):
                            kc = kcr * 8 + kc8
                            for mg in range(CH):
                                st = (half == 0 and kc == 0)
                                deps = [dd]
                                if st:
                                    deps += bank_free[mg] + rt_ready
                                mm_last[mg] = s.op(
                                    "pe", lambda E, mg=mg, kc=kc, kc8=kc8,
                                    h=half, wb=wb, RT=RT: E.matmul(
                                        PB[mg][:, 256 * h:256 * h + 256],
                                        RT[:, 512 * kc + 128 * mg:
                                           512 * kc + 128 * mg + 128],
                                        wb[:, 256 * kc8:256 * kc8 + 256],
                                        start=(h == 0 and kc == 0),
                                        stop=(h == 1 and kc == 15)),
                                    deps=deps,
                                    sig=((kc8 == 7 and mg == CH - 1)
                                         or (half == 1 and kc == 15)))
                        wb_free[wave % 2] = [mm_last[CH - 1]]
                        wave += 1
                evs = []
                for mg in range(CH):
                    ev = s.op("dve", lambda E, mg=mg: E.scalar_tensor_tensor(
                        out=SL1[:, 512 * mg:512 * mg + 512], in0=PB[mg][:],
                        scalar=CSC, in1=SL1[:, 512 * mg:512 * mg + 512],
                        op0=OP.mult, op1=OP.add),
                        deps=[mm_last[mg], xld[mg]], sig=True)
                    bank_free[mg] = [ev]
                    row0 = g * GT + mg * 128
                    stv = s.dma("sync",
                                xmid_d[row0:row0 + 128,
                                       512 * os:512 * os + 512],
                                SL1[:, 512 * mg:512 * mg + 512], deps=[ev])
                    evs.append(stv)
                sl1_free[0] = evs
                st_evs += evs
            return st_evs

        sl1_free = [[]]
        qw_guard = [[]]

        XT_cm = nc.sbuf_tensor("XT", [128, CH * D], F32)
        XT = XT_cm.__enter__()

        phase_A(0, XT)
        fence("a1")
        xt1_ready = s.barrier(("act",))[0]
        b1 = phase_B(0, XT, xt1_ready,
                     lambda base, fs: TV1[0:1, base + 512 * fs:
                                          base + 512 * fs + 512], [])
        dump("m1", M1[:, :], [128, CH])
        dump("i1", I1[:, :], [128, CH])
        dump("st", ST[:, :], [128, 8])
        dump("sl0a", SL0, [128, 2048])
        dump("yb_a", YB[:], [128, 2048])
        fence("b1")
        dump("xlnT1", XT[:], [128, CH * D])
        dump("phiq1", PHIQ[:], [128, CH * D])
        dump("phik1", PHIK[:], [128, CH * D])
        dump("kv1", KV[:], [128, CH * D])
        dump("tv1", TV1[0:1, :], [1, 2 * D])
        sp_evs = [s.dma("sync", spq_d[:, :], PHIQ[:], deps=b1),
                  s.dma("sync", spk_d[:, :], PHIK[:], deps=b1),
                  s.dma("sync", spkv_d[:, :], KV[:], deps=b1)]

        fence("spill")
        phase_A(1, XT)
        fence("a2")
        xt2_ready = s.barrier(("act",))[0]
        b2 = phase_B(1, XT, xt2_ready,
                     lambda base, fs: (CARKV if base == 0 else CARK)
                     [0:1, 512 * fs:512 * fs + 512], sp_evs)

        fence("b2")
        p1 = s.op("dve", lambda E: E.tensor_tensor(
            out=CARKV[0:1, :], in0=CARKV[0:1, :], in1=TV1[0:1, 0:D],
            op=OP.add), deps=b2 + b1, sig=True)
        p2 = s.op("dve", lambda E: E.tensor_tensor(
            out=CARK[0:1, :], in0=CARK[0:1, :], in1=TV1[0:1, D:2 * D],
            op=OP.add), deps=b2 + b1, sig=True)
        p3 = s.op("act", lambda E: E.activation(
            out=CARKV[0:1, :], in_=CARKV[0:1, :], func=AF.Copy,
            scale=FLAGE[0:1, 0:1]), deps=[p1, c_fe], sig=True)
        p4 = s.op("act", lambda E: E.activation(
            out=CARK[0:1, :], in_=CARK[0:1, :], func=AF.Copy,
            scale=FLAGE[0:1, 0:1]), deps=[p2], sig=True)
        di1 = s.dma("sync", spi_d[0:1, 0:D], CARKV[0:1, :], deps=[p3])
        di2 = s.dma("sync", spi_d[0:1, D:2 * D], CARK[0:1, :], deps=[p4])
        ccp = s.cc(lambda E: E.collective_compute(
            "AllReduce", OP.add,
            replica_groups=[[2 * i, 2 * i + 1] for i in range(4)],
            ins=[spi_d[:, :]], outs=[spo_d[:, :]]), deps=[di1, di2])
        do1 = s.dma("sync", CARKV[0:1, :], spo_d[0:1, 0:D], deps=[ccp])
        do2 = s.dma("sync", CARK[0:1, :], spo_d[0:1, D:2 * D], deps=[ccp])
        ci1, ci2 = do1, do2

        fence("cc1")
        dump("carkv2", CARKV[0:1, :], [1, D])
        dump("cark2", CARK[0:1, :], [1, D])
        r2 = phase_C(1, b2, [ci1, ci2], [
            (FLAGOR[:], FLAGO[0:1, 0:1],
             lambda nm, fs: (CARKV if nm == "kv" else CARK)
             [0:1, 512 * fs:512 * fs + 512]),
            (ONES1[:], ONES1[0:1, 0:1],
             lambda nm, fs: TV1[0:1, (0 if nm == "kv" else D) + 512 * fs:
                                (0 if nm == "kv" else D) + 512 * fs + 512]),
        ])
        fence("c2")
        c2_bar = s.barrier(("act", "pe", "dve", "pool"))
        dump("r2", PHIQ[:], [128, CH * D])
        d2 = phase_D(1, XT, r2)

        fence("d2")
        rl_evs = [s.dma("sync", PHIQ[:], spq_d[:, :],
                        deps=c2_bar + s.barrier(("act",))),
                  s.dma("sync", PHIK[:], spk_d[:, :], deps=c2_bar),
                  s.dma("sync", KV[:], spkv_d[:, :], deps=c2_bar)]
        do3 = s.dma("sync", CARKV[0:1, :], spo_d[0:1, 0:D], deps=c2_bar)
        do4 = s.dma("sync", CARK[0:1, :], spo_d[0:1, D:2 * D], deps=c2_bar)
        cj1, cj2 = do3, do4

        fence("rl")
        dump("rlq", PHIQ[:], [128, CH * D])
        dump("rlkv", KV[:], [128, CH * D])
        dump("rlk", PHIK[:], [128, CH * D])
        dump("carkv1", CARKV[0:1, :], [1, D])
        r1 = phase_C(0, rl_evs, [cj1, cj2], [
            (FLAGOR[:], FLAGO[0:1, 0:1],
             lambda nm, fs: (CARKV if nm == "kv" else CARK)
             [0:1, 512 * fs:512 * fs + 512]),
        ])
        fence("c1")
        dump("r1", PHIQ[:], [128, CH * D])
        d1 = phase_D(0, XT, r1)

        XT_cm.__exit__(None, None, None)

        fence("d1")
        e_bar = s.barrier(("dve", "act", "pe", "pool"))
        if "xmid" in DBG:
            dd = dbg_decl("xmid", [TOK, D])
            s.dma("sync", dd[:, :], xmid_d[:, :], deps=e_bar + d1 + d2)
        am_evs = []
        for mc in range(2 * CH):
            row0 = mc * 128
            ld = s.dma("sync", YB[:], xmid_d[row0:row0 + 128, :],
                       deps=yb_free[0] + e_bar + d1 + d2)
            iv = ln_stats(YB[:], M2[:, mc:mc + 1], I2[:, mc:mc + 1], [ld])
            ue = s.op("dve", lambda E, mc=mc: E.tensor_scalar(
                out=SL0, in0=YB[:], scalar1=M2[:, mc:mc + 1],
                scalar2=I2[:, mc:mc + 1], op0=OP.subtract, op1=OP.mult),
                deps=[iv], sig=True)
            yb_free[0] = [ue]
            am = s.op("dve", lambda E, mc=mc: E.tensor_reduce(
                out=AMX[:, mc:mc + 1], in_=SL0, axis=AX.X, op=OP.max,
                apply_absolute_value=True), deps=[ue], sig=True)
            am_evs.append(am)
        amx_ev = s.op("dve", lambda E: E.tensor_reduce(
            out=AMX[:, 0:1], in_=AMX[:, 0:8], axis=AX.X, op=OP.max,
            apply_absolute_value=True), deps=am_evs, sig=True)
        cp1 = s.op("pool", lambda E: E.tensor_reduce(
            out=SC[0:1, 0:1], in_=AMX[:, 0:1], axis=AX.C, op=OP.max),
            deps=[amx_ev], sig=True)
        cp2 = s.op("pool", lambda E: E.tensor_reduce(
            out=SC[0:1, 1:2], in_=AMW[:, 0:1], axis=AX.C, op=OP.max),
            deps=[amw_ev], sig=True)
        cb = s.op("dve", lambda E: E.tensor_tensor(
            out=SC[0:1, 2:3], in0=SC[0:1, 0:1], in1=SC[0:1, 1:2], op=OP.max),
            deps=[cp1, cp2], sig=True)
        bc1 = s.op("pool", lambda E: E.tensor_copy(
            CX[0:1, 0:8], SC[0:1, 0:1].broadcast_to((1, 8))), deps=[cp1],
            sig=True)
        bc2 = s.op("pool", lambda E: E.tensor_copy(
            CX[0:1, 8:16], SC[0:1, 2:3].broadcast_to((1, 8))), deps=[cb],
            sig=True)
        dci = s.dma("sync", cxi_d[:, :], CX[0:1, :], deps=[bc1, bc2])
        ccx = s.cc(lambda E: E.collective_compute(
            "AllReduce", OP.max, replica_groups=[list(range(NC))],
            ins=[cxi_d[:, :]], outs=[cxo_d[:, :]]), deps=[dci])
        dco = s.dma("sync", CX2[0:1, :], cxo_d[:, :], deps=[ccx])
        s1 = s.op("dve", lambda E: E.reciprocal(
            out=SC[0:1, 3:4], in_=CX2[0:1, 0:1]), deps=[dco], sig=True)
        s2 = s.op("dve", lambda E: E.tensor_scalar(
            out=SC[0:1, 4:5], in0=SC[0:1, 3:4], scalar1=K358, scalar2=None,
            op0=OP.mult), deps=[s1], sig=True)
        s3 = s.op("dve", lambda E: E.reciprocal(
            out=SC[0:1, 5:6], in_=CX2[0:1, 8:9]), deps=[dco], sig=True)
        s4 = s.op("dve", lambda E: E.tensor_scalar(
            out=SC[0:1, 6:7], in0=SC[0:1, 5:6], scalar1=K358, scalar2=None,
            op0=OP.mult), deps=[s3], sig=True)
        s5 = s.op("dve", lambda E: E.tensor_tensor(
            out=SC[0:1, 7:8], in0=CX2[0:1, 0:1], in1=CX2[0:1, 8:9],
            op=OP.mult), deps=[dco], sig=True)
        s6 = s.op("dve", lambda E: E.tensor_scalar(
            out=SC[0:1, 8:9], in0=SC[0:1, 7:8], scalar1=1.0 / (K358 * K358),
            scalar2=None, op0=OP.mult), deps=[s5], sig=True)
        s7 = s.op("dve", lambda E: E.reciprocal(
            out=SC[0:1, 9:10], in_=SC[0:1, 8:9]), deps=[s6], sig=True)
        bsc = s.op("act", lambda E: E.activation(
            out=BIASR[0:1, :], in_=BIASR[0:1, :], func=AF.Copy,
            scale=SC[0:1, 9:10]), deps=[s7, c_br], sig=True)
        bevs = []
        for (col, src_ev, scol) in ((1, s2, 4), (2, s4, 6), (3, s6, 8)):
            mm = s.op("pe", lambda E, scol=scol: E.matmul(
                PB[5][:, 0:1], ONES1[:], SC[0:1, scol:scol + 1],
                start=True, stop=True),
                deps=[src_ev, c_o1] + bank_free[5], sig=True)
            cp = s.op("act", lambda E, col=col: E.copy(
                out=B128[:, col:col + 1], in_=PB[5][:, 0:1]), deps=[mm],
                sig=True)
            bank_free[5] = [cp]
            bevs.append(cp)

        def emit_quant(src_ap, out_ap, scale_col, sA, sB, sC, sD, deps):
            e1 = s.op("act", lambda E: E.activation(
                out=sA, in_=src_ap, func=AF.Copy,
                scale=B128[:, scale_col:scale_col + 1]), deps=deps, sig=True)
            e2 = s.op("dve", lambda E: E.scalar_tensor_tensor(
                out=sC, in0=sA, scalar=PHI4_32, in1=sA,
                op0=OP.mult, op1=OP.subtract), deps=[e1], sig=True)
            e3 = s.op("dve", lambda E: E.scalar_tensor_tensor(
                out=sB, in0=sA, scalar=PHI4_32, in1=sC,
                op0=OP.mult, op1=OP.subtract), deps=[e2], sig=True)
            e4 = s.op("pool", lambda E: E.tensor_scalar(
                out=sC, in0=sA, scalar1=PHI4_64, scalar2=None, op0=OP.mult),
                deps=[e3], sig=True)
            e5 = s.op("pool", lambda E: E.tensor_tensor(
                out=sD, in0=sC, in1=sA, op=OP.subtract), deps=[e4], sig=True)
            e6 = s.op("pool", lambda E: E.tensor_tensor(
                out=sC, in0=sC, in1=sD, op=OP.subtract), deps=[e5], sig=True)
            e7 = s.op("pool", lambda E: E.tensor_tensor(
                out=sC, in0=sC, in1=sB, op=OP.subtract), deps=[e6], sig=True)
            return s.op("dve", lambda E: E.scalar_tensor_tensor(
                out=out_ap, in0=sC, scalar=B128[:, 0:1], in1=sB,
                op0=OP.mult, op1=OP.add), deps=[e7, ind_ev], sig=True)

        fence("e")
        dump("sc", SC[0:1, :], [1, 16])
        dump("cx2", CX2[0:1, :], [1, 16])
        dump("b128", B128[:, :], [128, 4])
        dump("m2", M2[:, :], [128, 2 * CH])
        dump("i2", I2[:, :], [128, 2 * CH])
        QX_cm = nc.sbuf_tensor("QX", [128, 16 * TOK], BF16)
        QX = QX_cm.__enter__()

        fence("scales")
        qx_evs = []
        for mc in range(2 * CH):
            row0 = mc * 128
            ld = s.dma("sync", YB[:], xmid_d[row0:row0 + 128, :],
                       deps=yb_free[0] + d1 + d2)
            ue = s.op("dve", lambda E, mc=mc: E.tensor_scalar(
                out=SL0, in0=YB[:], scalar1=M2[:, mc:mc + 1],
                scalar2=I2[:, mc:mc + 1], op0=OP.subtract, op1=OP.mult),
                deps=[ld] + qx_evs[-4:], sig=True)
            ut_evs = transpose_to(
                lambda j: SL0[:, 128 * j:128 * j + 128],
                lambda jb: SL0[:, 512 * jb:512 * jb + 512],
                [ue])
            last_q = []
            for qq in range(4):
                srcq = SL0[:, 512 * qq:512 * qq + 512]
                dst = bass.AP(QX, 4096 * qq + 128 * mc,
                              [[16 * TOK, 128], [TOK, 4], [1, 128]])
                e7 = emit_quant(srcq, dst, 1,
                                YB[:, 0:512], YB[:, 512:1024],
                                YB[:, 1024:1536], YB[:, 1536:2048],
                                [ut_evs[qq], bevs[0]] + last_q)
                last_q = [e7]
            qx_evs += last_q
            yb_free[0] = last_q
        fence("qx")
        qx_done = s.barrier(("dve", "pool", "act"))
        dump("qx", QX[:], [128, 16 * TOK], BF16)

        for os in range(4):
            fence(f"f{os}")
            fill_evs = [[] for _ in range(2 * CH)]
            wave = 0
            for half in range(2):
                for kcr in range(2):
                    wb = WB[wave % 2]
                    src = bass.AP(
                        wffn_d, os * 512 + 256 * half + kcr * 8 * 128 * D,
                        [[D, 128], [128 * D, 8], [1, 256]])
                    dd = s.dma("sync",
                               wb[:].rearrange("p (k c) -> p k c", k=8), src,
                               deps=wb_free[wave % 2] +
                               (qx_done if os == 0 else []))
                    qdone = []
                    for qq in range(4):
                        srcq = wb[:, 512 * qq:512 * qq + 512]
                        w7 = emit_quant(
                            srcq, QW[:, 512 * qq:512 * qq + 512], 2,
                            SL0[:, 0:512], SL0[:, 512:1024],
                            SL0[:, 1024:1536], SL0[:, 1536:2048],
                            [dd, bevs[1]] + qdone[-1:] + qx_done
                            + (qw_guard[0] if qq == 0 else []))
                        qdone.append(w7)
                    mm_last = [None] * (2 * CH)
                    for kc8 in range(8):
                        kc = kcr * 8 + kc8
                        for mc in range(2 * CH):
                            st = (half == 0 and kc == 0)
                            deps = [qdone[kc8 // 2]]
                            if st:
                                deps += bank_free[mc]
                            mm_last[mc] = s.op(
                                "pe", lambda E, mc=mc, kc=kc, kc8=kc8, h=half:
                                E.matmul(
                                    PB[mc][:, 256 * h:256 * h + 256],
                                    QX[:, TOK * kc + 128 * mc:
                                       TOK * kc + 128 * mc + 128],
                                    QW[:, 256 * kc8:256 * kc8 + 256],
                                    start=(h == 0 and kc == 0), stop=False),
                                deps=deps,
                                sig=((kc8 == 7 and mc == 2 * CH - 1)
                                     or (half == 1 and kc == 15)))
                    wb_free[wave % 2] = [mm_last[2 * CH - 1]]
                    for mc in range(2 * CH):
                        fill_evs[mc].append(mm_last[mc])
                    qw_guard[0] = [mm_last[2 * CH - 1]]
                    wave += 1
            for mc in range(2 * CH):
                for half in range(2):
                    bm = s.op("pe", lambda E, mc=mc, os=os, h=half: E.matmul(
                        PB[mc][:, 256 * h:256 * h + 256], ONES1[:],
                        BIASR[0:1, 512 * os + 256 * h:
                              512 * os + 256 * h + 256],
                        start=False, stop=(h == 1)),
                        deps=[bsc] + fill_evs[mc], sig=(half == 1))
                    fill_evs[mc].append(bm)
            for mc in range(2 * CH):
                row0 = mc * 128
                xld = s.dma("sync", YB[:, 512 * os:512 * os + 512],
                            xmid_d[row0:row0 + 128, 512 * os:512 * os + 512],
                            deps=yb_free[0] + d1 + d2)
                ev = s.op("dve", lambda E, mc=mc, os=os:
                          E.scalar_tensor_tensor(
                              out=YB[:, 512 * os:512 * os + 512],
                              in0=PB[mc][:], scalar=B128[:, 3:4],
                              in1=YB[:, 512 * os:512 * os + 512],
                              op0=OP.mult, op1=OP.add),
                          deps=[xld, bevs[2]] + fill_evs[mc][-2:], sig=True)
                bank_free[mc] = [ev]
                sto = s.dma("sync",
                            y_d[row0:row0 + 128, 512 * os:512 * os + 512],
                            YB[:, 512 * os:512 * os + 512], deps=[ev])
                yb_free[0] = [sto]

        QX_cm.__exit__(None, None, None)

        last_out = ("dpool", s.cnt["dpool"])
        s.op("pool", lambda E: E.engine_nop(), deps=[last_out])

        with nc.Block() as block:
            s.emit(block, semh)
    return nc


_NC_CACHE = {}


def kernel(L0_attn_x, L0_attn_W_qkv, L0_attn_W_out, L0_ffn_weight,
           L0_ffn_bias):
    x = np.ascontiguousarray(
        np.asarray(L0_attn_x, np.float32)).reshape(NC * TOK, D)
    wqkvT = np.ascontiguousarray(np.asarray(L0_attn_W_qkv, np.float32).T)
    woutT = np.ascontiguousarray(np.asarray(L0_attn_W_out, np.float32).T)
    wffnT = np.ascontiguousarray(np.asarray(L0_ffn_weight, np.float32).T)
    biasr = np.ascontiguousarray(
        np.asarray(L0_ffn_bias, np.float32)).reshape(1, D)
    tri = np.triu(np.ones((128, 128), np.float32))
    ident = np.eye(128, dtype=np.float32)
    ones1 = np.ones((1, 128), np.float32)
    onesc = np.ones((128, 1), np.float32)

    if "nc" not in _NC_CACHE:
        _NC_CACHE["nc"] = build_program()
    nc = _NC_CACHE["nc"]
    in_maps = []
    for c in range(NC):
        in_maps.append({
            "x": x[c * TOK:(c + 1) * TOK],
            "wqkvT": wqkvT, "woutT": woutT, "wffnT": wffnT, "biasr": biasr,
            "tri": tri, "ident": ident, "ones1": ones1, "onesc": onesc,
            "flago": np.full((1, 1), float(c % 2), np.float32),
            "flagor": np.full((1, 128), float(c % 2), np.float32),
            "flage": np.full((1, 1), float(1 - c % 2), np.float32),
        })
    res = run_bass_kernel_spmd(nc, in_maps, core_ids=list(range(NC)))
    y = np.concatenate([np.asarray(r["y"]) for r in res.results], axis=0)
    return y.reshape(4, 2048, 2048)


# version 04
# speedup vs baseline: 1.0208x; 1.0208x over previous
import numpy as np
import concourse.bass as bass
import concourse.mybir as mybir
from concourse.bass_utils import run_bass_kernel_spmd

F32 = mybir.dt.float32
BF16 = mybir.dt.bfloat16
AF = mybir.ActivationFunctionType
OP = mybir.AluOpType
AX = mybir.AxisListType

D = 2048
TOK = 1024
G = 2
GT = TOK // G
CH = GT // 128
NC = 8

PHI4_32 = float(2 ** 20 + 1)
PHI4_64 = float(2 ** 49 + 1)
K358 = 448.0 * 0.8
EPS_LN = 1e-5
EPS_ATTN = 1e-9
CSC = 128.0 ** -0.5


class Sched:
    ENG2BLOCK = {"pe": "tensor", "dve": "vector", "act": "scalar",
                 "pool": "gpsimd", "sync": "sync"}

    def __init__(self):
        self.q = {e: [] for e in self.ENG2BLOCK}
        self.cnt = {"pe": 0, "dve": 0, "act": 0, "pool": 0,
                    "dsync": 0, "dpool": 0, "cc": 0}
        self.gate = []

    def op(self, eng, fn, deps=(), sig=False):
        deps = [d for d in deps if d is not None] + self.gate
        ev = None
        if sig:
            self.cnt[eng] += 1
            ev = (eng, self.cnt[eng])
        self.q[eng].append([deps, fn, ("c", ev) if ev else None])
        return ev

    def dma(self, eng, out, in_, deps=()):
        sem = "dsync" if eng == "sync" else "dpool"
        deps = [d for d in deps if d is not None] + self.gate
        self.cnt[sem] += 16
        ev = (sem, self.cnt[sem])
        self.q[eng].append(
            [deps, (lambda E, o=out, i=in_: E.dma_start(out=o, in_=i)),
             ("d", ev)])
        return ev

    def cc(self, fn, deps=()):
        deps = [d for d in deps if d is not None]
        self.cnt["cc"] += 1
        ev = ("cc", self.cnt["cc"])
        self.q["pool"].append([deps, fn, ("c", ev)])
        return ev

    def barrier(self, engines=("pe", "dve", "act", "pool")):
        evs = []
        for e in engines:
            if not self.q[e]:
                continue
            last = self.q[e][-1]
            if last[2] is None:
                self.cnt[e] += 1
                ev = (e, self.cnt[e])
                last[2] = ("c", ev)
            else:
                ev = last[2][1]
            evs.append(ev)
        return evs

    def emit(self, block, semh):
        for eng, bname in self.ENG2BLOCK.items():
            items = self.q[eng]
            own = eng

            def body(E, items=items, own=own):
                waited = {}
                for deps, fn, sig in items:
                    for (sname, v) in deps:
                        if waited.get(sname, 0) < v:
                            E.wait_ge(semh[sname], v)
                            waited[sname] = v
                    ins = fn(E)
                    if sig is not None:
                        kind, (sname, v) = sig
                        ins.then_inc(semh[sname], 16 if kind == "d" else 1)

            getattr(block, bname)(body)


DBG = set()
FENCES = set()


def build_program():
    nc = bass.Bass("TRN2", target_bir_lowering=False,
                   dynamic_dma_scratch_size=4096)
    dbg_outs = {}

    def dbg_decl(name, shape, dt=F32):
        dbg_outs[name] = nc.dram_tensor("dbg_" + name, shape, dt,
                                        kind="ExternalOutput")
        return dbg_outs[name]

    x_d = nc.dram_tensor("x", [TOK, D], F32, kind="ExternalInput")
    wqkv_d = nc.dram_tensor("wqkvT", [D, 3 * D], F32, kind="ExternalInput")
    wout_d = nc.dram_tensor("woutT", [D, D], F32, kind="ExternalInput")
    wffn_d = nc.dram_tensor("wffnT", [D, D], F32, kind="ExternalInput")
    bias_d = nc.dram_tensor("biasr", [1, D], F32, kind="ExternalInput")
    tri_d = nc.dram_tensor("tri", [128, 128], F32, kind="ExternalInput")
    id_d = nc.dram_tensor("ident", [128, 128], F32, kind="ExternalInput")
    ones1_d = nc.dram_tensor("ones1", [1, 128], F32, kind="ExternalInput")
    onesc_d = nc.dram_tensor("onesc", [128, 1], F32, kind="ExternalInput")
    flago_d = nc.dram_tensor("flago", [1, 1], F32, kind="ExternalInput")
    flagor_d = nc.dram_tensor("flagor", [1, 128], F32, kind="ExternalInput")
    flage_d = nc.dram_tensor("flage", [1, 1], F32, kind="ExternalInput")
    y_d = nc.dram_tensor("y", [TOK, D], F32, kind="ExternalOutput")
    xmid_d = nc.dram_tensor("xmid_i", [TOK, D], F32)
    spq_d = nc.dram_tensor("spq_i", [128, CH * D], F32)
    spk_d = nc.dram_tensor("spk_i", [128, CH * D], F32)
    spkv_d = nc.dram_tensor("spkv_i", [128, CH * D], F32)
    spi_d = nc.dram_tensor("spi_i", [1, 2 * D], F32)
    spo_d = nc.dram_tensor("spo_i", [1, 2 * D], F32)
    cxi_d = nc.dram_tensor("cxi_i", [1, 16], F32)
    cxo_d = nc.dram_tensor("cxo_i", [1, 16], F32, addr_space="Shared")

    from contextlib import ExitStack
    s = Sched()
    with ExitStack() as es:
        e = es.enter_context
        PHIQ = e(nc.sbuf_tensor("PHIQ", [128, CH * D], F32))
        KV = e(nc.sbuf_tensor("KV", [128, CH * D], F32))
        PHIK = e(nc.sbuf_tensor("PHIK", [128, CH * D], F32))
        WB = [e(nc.sbuf_tensor(f"WB{i}", [128, 4096], F32)) for i in range(2)]
        QSCR = e(nc.sbuf_tensor("QSCR", [128, 4096], F32))
        YB = e(nc.sbuf_tensor("YB", [128, D], F32))
        TRI = e(nc.sbuf_tensor("TRI", [128, 128], F32))
        IDN = e(nc.sbuf_tensor("IDN", [128, 128], F32))
        ONES1 = e(nc.sbuf_tensor("ONES1", [1, 128], F32))
        ONESC = e(nc.sbuf_tensor("ONESC", [128, 1], F32))
        FLAGO = e(nc.sbuf_tensor("FLAGO", [1, 1], F32))
        FLAGOR = e(nc.sbuf_tensor("FLAGOR", [1, 128], F32))
        FLAGE = e(nc.sbuf_tensor("FLAGE", [1, 1], F32))
        CX = e(nc.sbuf_tensor("CX", [1, 16], F32))
        CX2 = e(nc.sbuf_tensor("CX2", [1, 16], F32))
        SC = e(nc.sbuf_tensor("SC", [1, 16], F32))
        ST = e(nc.sbuf_tensor("ST", [128, 8], F32))
        M1 = e(nc.sbuf_tensor("M1", [128, CH], F32))
        I1 = e(nc.sbuf_tensor("I1", [128, CH], F32))
        M2 = e(nc.sbuf_tensor("M2", [128, 2 * CH], F32))
        I2 = e(nc.sbuf_tensor("I2", [128, 2 * CH], F32))
        AMX = e(nc.sbuf_tensor("AMX", [128, 8], F32))
        AMW = e(nc.sbuf_tensor("AMW", [128, 16], F32))
        B128 = e(nc.sbuf_tensor("B128", [128, 4], F32))
        PB = [e(nc.psum_tensor(f"PB{i}", [128, 512], F32)) for i in range(8)]
        semh = {n: e(nc.semaphore(n)) for n in
                ("pe", "dve", "act", "pool", "dsync", "dpool", "cc")}

        SL0 = QSCR[:, 0:2048]
        SL1 = QSCR[:, 2048:4096]

        bank_free = [[] for _ in range(8)]
        wb_free = [[], []]
        yb_free = [[]]

        def fence(name):
            if name != "rl" and "ALL" not in FENCES and name not in FENCES:
                return
            evs = s.barrier(("pe", "dve", "act", "pool"))
            evs += [("dsync", s.cnt["dsync"]), ("dpool", s.cnt["dpool"])]
            if s.cnt["cc"]:
                evs.append(("cc", s.cnt["cc"]))
            s.gate = evs

        def dump(name, src_ap, shape, dt=F32):
            if name not in DBG:
                return
            dd = dbg_decl(name, shape, dt)
            ev = s.dma("sync", dd[tuple(slice(0, x) for x in shape)], src_ap,
                       deps=s.barrier(("pe", "dve", "act", "pool")))
            s.gate = [ev]

        c_tri = s.dma("sync", TRI[:], tri_d[:, :])
        c_id = s.dma("sync", IDN[:], id_d[:, :])
        c_o1 = s.dma("sync", ONES1[:], ones1_d[:, :])
        c_oc = s.dma("sync", ONESC[:], onesc_d[:, :])
        c_fo = s.dma("sync", FLAGO[:], flago_d[:, :])
        c_for = s.dma("sync", FLAGOR[:], flagor_d[:, :])
        c_fe = s.dma("sync", FLAGE[:], flage_d[:, :])

        iv0 = s.op("dve", lambda E: E.memset(B128[:, 0:1], 1.0), sig=True)
        iv1 = s.op("dve", lambda E: E.tensor_scalar(
            out=B128[:, 0:1], in0=B128[:, 0:1], scalar1=float(2.0 ** -30),
            scalar2=None, op0=OP.add), deps=[iv0], sig=True)
        ind_ev = s.op("dve", lambda E: E.tensor_scalar(
            out=B128[:, 0:1], in0=B128[:, 0:1], scalar1=1.0,
            scalar2=float(2.0 ** 30), op0=OP.subtract, op1=OP.mult),
            deps=[iv1], sig=True)

        amw_parts = []
        for i in range(8):
            src = bass.AP(wffn_d, i * 256, [[D, 128], [128 * D, 16], [1, 256]])
            dst = WB[i % 2][:].rearrange("p (k c) -> p k c", k=16)
            ld = s.dma("sync", dst, src, deps=wb_free[i % 2])
            rd = s.op("dve", lambda E, i=i, b=i % 2: E.tensor_reduce(
                out=AMW[:, i:i + 1], in_=WB[b][:], axis=AX.X, op=OP.max,
                apply_absolute_value=True), deps=[ld], sig=True)
            wb_free[i % 2] = [rd]
            amw_parts.append(rd)
        amw_ev = s.op("dve", lambda E: E.tensor_reduce(
            out=AMW[:, 0:1], in_=AMW[:, 0:8], axis=AX.X, op=OP.max,
            apply_absolute_value=True), deps=amw_parts, sig=True)

        def ln_stats(chunk_src, mean_ap, inv_ap, deps):
            r1 = s.op("dve", lambda E: E.tensor_reduce(
                out=ST[:, 0:1], in_=chunk_src, axis=AX.X, op=OP.add),
                deps=deps, sig=True)
            sq = s.op("act", lambda E: E.activation(
                out=SL1, in_=chunk_src, func=AF.Square), deps=deps, sig=True)
            r2 = s.op("dve", lambda E: E.tensor_reduce(
                out=ST[:, 1:2], in_=SL1, axis=AX.X, op=OP.add),
                deps=[sq], sig=True)
            g1 = s.op("dve", lambda E: E.tensor_scalar(
                out=mean_ap, in0=ST[:, 0:1], scalar1=1.0 / D, scalar2=None,
                op0=OP.mult), deps=[r1], sig=True)
            g2 = s.op("dve", lambda E: E.tensor_scalar(
                out=ST[:, 2:3], in0=ST[:, 1:2], scalar1=1.0 / D, scalar2=None,
                op0=OP.mult), deps=[r2], sig=True)
            g3 = s.op("dve", lambda E: E.tensor_tensor(
                out=ST[:, 3:4], in0=mean_ap, in1=mean_ap, op=OP.mult),
                deps=[g1], sig=True)
            g4 = s.op("dve", lambda E: E.tensor_tensor(
                out=ST[:, 4:5], in0=ST[:, 2:3], in1=ST[:, 3:4],
                op=OP.subtract), deps=[g2, g3], sig=True)
            ve = s.op("dve", lambda E: E.tensor_scalar(
                out=ST[:, 5:6], in0=ST[:, 4:5], scalar1=EPS_LN, scalar2=None,
                op0=OP.add), deps=[g4], sig=True)
            sd = s.op("act", lambda E: E.activation(
                out=ST[:, 6:7], in_=ST[:, 5:6], func=AF.Sqrt), deps=[ve],
                sig=True)
            return s.op("dve", lambda E: E.reciprocal(
                out=inv_ap, in_=ST[:, 6:7]), deps=[sd], sig=True)

        def transpose_to(src_tile_fn, dst_fn, deps):
            evs = []
            for jb in range(4):
                bidx = 6 + (jb % 2)
                bank = PB[bidx]
                last = None
                for t in range(4):
                    j = 4 * jb + t
                    dd = (deps + bank_free[bidx] + [c_id]) if t == 0 else []
                    last = s.op("pe", lambda E, j=j, t=t, b=bank: E.transpose(
                        b[:, 128 * t:128 * t + 128], src_tile_fn(j), IDN[:]),
                        deps=dd, sig=(t == 3))
                ev = s.op("act", lambda E, jb=jb, b=bank: E.copy(
                    out=dst_fn(jb), in_=b[:, 0:512]), deps=[last], sig=True)
                bank_free[bidx] = [ev]
                evs.append(ev)
            return evs

        def phase_A(g, XT):
            tp_evs = []
            for mg in range(CH):
                row0 = g * GT + mg * 128
                ld = s.dma("sync", YB[:], x_d[row0:row0 + 128, :],
                           deps=yb_free[0])
                iv = ln_stats(YB[:], M1[:, mg:mg + 1], I1[:, mg:mg + 1], [ld])
                ap = s.op("dve", lambda E, mg=mg: E.tensor_scalar(
                    out=SL0, in0=YB[:], scalar1=M1[:, mg:mg + 1],
                    scalar2=I1[:, mg:mg + 1], op0=OP.subtract, op1=OP.mult),
                    deps=[iv] + tp_evs[-4:], sig=True)
                yb_free[0] = [ap]
                tp_evs = transpose_to(
                    lambda j: SL0[:, 128 * j:128 * j + 128],
                    lambda jb, mg=mg: bass.AP(
                        XT, 2048 * jb + 128 * mg,
                        [[CH * D, 128], [512, 4], [1, 128]]),
                    [ap])
            return tp_evs

        def phase_B(g, XT, xt_ready, totg_fn, extra):
            for fs in range(4):
                phk_evs = [None] * CH
                kv_evs = [None] * CH
                for tgt in range(3):
                    col0 = tgt * D + fs * 512
                    mm_last = [None] * CH
                    for kcr in range(2):
                        wb = WB[kcr]
                        src = bass.AP(
                            wqkv_d, col0 + kcr * 8 * 128 * 3 * D,
                            [[3 * D, 128], [128 * 3 * D, 8], [1, 512]])
                        dd = s.dma("sync",
                                   wb[:].rearrange("p (k c) -> p k c", k=8),
                                   src, deps=wb_free[kcr])
                        for kc8 in range(8):
                            kc = kcr * 8 + kc8
                            for mg in range(CH):
                                st = (kc == 0)
                                deps = [dd]
                                if st:
                                    deps += bank_free[mg] + [xt_ready]
                                mm_last[mg] = s.op(
                                    "pe",
                                    lambda E, mg=mg, kc=kc, kc8=kc8,
                                    wb=wb, XT=XT: E.matmul(
                                        PB[mg][:],
                                        XT[:, 512 * kc + 128 * mg:
                                           512 * kc + 128 * mg + 128],
                                        wb[:, 512 * kc8:512 * kc8 + 512],
                                        start=(kc == 0), stop=(kc == 15)),
                                    deps=deps,
                                    sig=((kc8 == 7 and mg == CH - 1)
                                         or kc == 15))
                        wb_free[kcr] = [mm_last[CH - 1]]
                    for mg in range(CH):
                        bank = PB[mg]
                        dsl = slice(D * mg + 512 * fs, D * mg + 512 * fs + 512)
                        if tgt < 2:
                            out = PHIQ if tgt == 0 else PHIK
                            t1 = s.op("dve", lambda E, bank=bank: E.tensor_scalar(
                                out=SL0[:, 0:512], in0=bank[:], scalar1=0.0,
                                scalar2=None, op0=OP.min),
                                deps=[mm_last[mg]], sig=True)
                            ex = s.op("act", lambda E: E.activation(
                                out=SL0[:, 512:1024], in_=SL0[:, 0:512],
                                func=AF.Exp), deps=[t1], sig=True)
                            rl = s.op("act", lambda E, bank=bank: E.activation(
                                out=SL0[:, 1024:1536], in_=bank[:],
                                func=AF.Relu), deps=[mm_last[mg]], sig=True)
                            ph = s.op("dve",
                                      lambda E, out=out, dsl=dsl: E.tensor_tensor(
                                          out=out[:, dsl], in0=SL0[:, 512:1024],
                                          in1=SL0[:, 1024:1536], op=OP.add),
                                      deps=[ex, rl] + extra, sig=True)
                            bank_free[mg] = [t1, rl]
                            if tgt == 1:
                                phk_evs[mg] = ph
                        else:
                            kve = s.op("dve",
                                       lambda E, dsl=dsl, bank=bank:
                                       E.tensor_tensor(
                                           out=KV[:, dsl], in0=PHIK[:, dsl],
                                           in1=bank[:], op=OP.mult),
                                       deps=[mm_last[mg], phk_evs[mg]] + extra,
                                       sig=True)
                            bank_free[mg] = [kve]
                            kv_evs[mg] = kve
                for (tens, base, sevs) in ((KV, 0, kv_evs), (PHIK, D, phk_evs)):
                    mms = []
                    for mg in range(CH):
                        mms.append(s.op(
                            "pe", lambda E, tens=tens, mg=mg, fs=fs: E.matmul(
                                PB[4][0:1, 0:512], ONESC[:],
                                tens[:, D * mg + 512 * fs:
                                     D * mg + 512 * fs + 512],
                                start=(mg == 0), stop=(mg == CH - 1)),
                            deps=(sevs + bank_free[4] + [c_oc])
                            if mg == 0 else [], sig=(mg == CH - 1)))
                    tev = s.op("act", lambda E, base=base, fs=fs,
                               totg_fn=totg_fn: E.copy(
                                   out=totg_fn(base, fs), in_=PB[4][0:1, 0:512]),
                               deps=[mms[-1]], sig=True)
                    bank_free[4] = [tev]
            return s.barrier(("dve", "act", "pe"))

        def phase_C(g, src_deps, car_init, car0):
            r_evs = []
            car_ev = {}
            for fs in range(4):
                car_ev[("kv", fs)] = list(car_init)
                car_ev[("k", fs)] = list(car_init)
            for mg in range(CH):
                for fs in range(4):
                    sl = slice(D * mg + 512 * fs, D * mg + 512 * fs + 512)
                    evd = {}
                    for nm, tens, car, bi in (("kv", KV, CARKV, 4),
                                              ("k", PHIK, CARK, 5)):
                        bc = bi + 2
                        s.op("pe", lambda E, tens=tens, sl=sl, bi=bi:
                             E.matmul(PB[bi][:], TRI[:], tens[:, sl],
                                      start=True, stop=False),
                             deps=bank_free[bi] + src_deps + [c_tri])
                        if mg == 0:
                            terms = [(row, scl, rfn(nm, fs))
                                     for (row, scl, rfn) in car0]
                        else:
                            terms = [(ONES1[:], ONES1[0:1, 0:1],
                                      car[0:1, 512 * fs:512 * fs + 512])]
                        t2 = t3 = None
                        nt = len(terms)
                        for ti, (row, scl, rhs) in enumerate(terms):
                            t2 = s.op("pe", lambda E, row=row, rhs=rhs, bi=bi,
                                      last=(ti == nt - 1):
                                      E.matmul(PB[bi][:], row, rhs,
                                               start=False, stop=last),
                                      deps=car_ev[(nm, fs)] + [c_o1, c_for],
                                      sig=(ti == nt - 1))
                        s.op("pe", lambda E, tens=tens, sl=sl, bc=bc:
                             E.matmul(PB[bc][0:1, 0:512], ONESC[:],
                                      tens[:, sl], start=True, stop=False),
                             deps=bank_free[bc] + [c_oc])
                        for ti, (row, scl, rhs) in enumerate(terms):
                            t3 = s.op("pe", lambda E, scl=scl, rhs=rhs, bc=bc,
                                      last=(ti == nt - 1):
                                      E.matmul(PB[bc][0:1, 0:512], scl, rhs,
                                               start=False, stop=last),
                                      deps=car_ev[(nm, fs)],
                                      sig=(ti == nt - 1))
                        cx = s.op("act", lambda E, car=car, fs=fs, bc=bc:
                                  E.copy(out=car[0:1, 512 * fs:512 * fs + 512],
                                         in_=PB[bc][0:1, 0:512]),
                                  deps=[t3, t2], sig=True)
                        bank_free[bc] = [cx]
                        car_ev[(nm, fs)] = [cx]
                        evd[nm] = t2
                        evd[nm + "x"] = cx
                    num = s.op("dve", lambda E, sl=sl: E.tensor_tensor(
                        out=SL0[:, 0:512], in0=PHIQ[:, sl], in1=PB[4][:],
                        op=OP.mult), deps=[evd["kv"]], sig=True)
                    den = s.op("dve", lambda E, sl=sl: E.tensor_tensor(
                        out=SL0[:, 512:1024], in0=PHIQ[:, sl], in1=PB[5][:],
                        op=OP.mult), deps=[evd["k"]], sig=True)
                    bank_free[4] = [num, evd["kvx"]]
                    bank_free[5] = [den, evd["kx"]]
                    de = s.op("pool", lambda E: E.tensor_scalar(
                        out=SL0[:, 512:1024], in0=SL0[:, 512:1024],
                        scalar1=EPS_ATTN, scalar2=None, op0=OP.add),
                        deps=[den], sig=True)
                    rc = s.op("dve", lambda E: E.reciprocal(
                        out=SL0[:, 1024:1536], in_=SL0[:, 512:1024]),
                        deps=[de], sig=True)
                    rv = s.op("dve", lambda E, sl=sl: E.tensor_tensor(
                        out=PHIQ[:, sl], in0=SL0[:, 0:512],
                        in1=SL0[:, 1024:1536], op=OP.mult), deps=[rc], sig=True)
                    r_evs.append(rv)
            return r_evs

        def phase_D(g, RT, r_deps):
            tp = []
            for mg in range(CH):
                tp += transpose_to(
                    lambda j, mg=mg: PHIQ[:, D * mg + 128 * j:
                                          D * mg + 128 * j + 128],
                    lambda jb, mg=mg: bass.AP(
                        RT, 2048 * jb + 128 * mg,
                        [[CH * D, 128], [512, 4], [1, 128]]),
                    r_deps)
            rt_ready = s.barrier(("act",))
            st_evs = []
            for os in range(4):
                xld = [None] * CH
                for mg in range(CH):
                    row0 = g * GT + mg * 128
                    xld[mg] = s.dma(
                        "sync", SL1[:, 512 * mg:512 * mg + 512],
                        x_d[row0:row0 + 128, 512 * os:512 * os + 512],
                        deps=sl1_free[0])
                mm_last = [None] * CH
                for kcr in range(2):
                    wb = WB[kcr]
                    src = bass.AP(
                        wout_d, os * 512 + kcr * 8 * 128 * D,
                        [[D, 128], [128 * D, 8], [1, 512]])
                    dd = s.dma("sync",
                               wb[:].rearrange("p (k c) -> p k c", k=8),
                               src, deps=wb_free[kcr])
                    for kc8 in range(8):
                        kc = kcr * 8 + kc8
                        for mg in range(CH):
                            st = (kc == 0)
                            deps = [dd]
                            if st:
                                deps += bank_free[mg] + rt_ready
                            mm_last[mg] = s.op(
                                "pe", lambda E, mg=mg, kc=kc, kc8=kc8,
                                wb=wb, RT=RT: E.matmul(
                                    PB[mg][:],
                                    RT[:, 512 * kc + 128 * mg:
                                       512 * kc + 128 * mg + 128],
                                    wb[:, 512 * kc8:512 * kc8 + 512],
                                    start=(kc == 0), stop=(kc == 15)),
                                deps=deps,
                                sig=((kc8 == 7 and mg == CH - 1)
                                     or kc == 15))
                    wb_free[kcr] = [mm_last[CH - 1]]
                evs = []
                for mg in range(CH):
                    ev = s.op("dve", lambda E, mg=mg: E.scalar_tensor_tensor(
                        out=SL1[:, 512 * mg:512 * mg + 512], in0=PB[mg][:],
                        scalar=CSC, in1=SL1[:, 512 * mg:512 * mg + 512],
                        op0=OP.mult, op1=OP.add),
                        deps=[mm_last[mg], xld[mg]], sig=True)
                    bank_free[mg] = [ev]
                    row0 = g * GT + mg * 128
                    stv = s.dma("sync",
                                xmid_d[row0:row0 + 128,
                                       512 * os:512 * os + 512],
                                SL1[:, 512 * mg:512 * mg + 512], deps=[ev])
                    evs.append(stv)
                sl1_free[0] = evs
                st_evs += evs
            return st_evs

        sl1_free = [[]]
        qw_guard = [[]]

        TV1_cm = nc.sbuf_tensor("TV1", [1, 2 * D], F32)
        TV1 = TV1_cm.__enter__()
        CARKV_cm = nc.sbuf_tensor("CARKV", [1, D], F32)
        CARKV = CARKV_cm.__enter__()
        CARK_cm = nc.sbuf_tensor("CARK", [1, D], F32)
        CARK = CARK_cm.__enter__()
        XT_cm = nc.sbuf_tensor("XT", [128, CH * D], F32)
        XT = XT_cm.__enter__()

        phase_A(0, XT)
        fence("a1")
        xt1_ready = s.barrier(("act",))[0]
        b1 = phase_B(0, XT, xt1_ready,
                     lambda base, fs: TV1[0:1, base + 512 * fs:
                                          base + 512 * fs + 512], [])
        dump("m1", M1[:, :], [128, CH])
        dump("i1", I1[:, :], [128, CH])
        dump("st", ST[:, :], [128, 8])
        dump("sl0a", SL0, [128, 2048])
        dump("yb_a", YB[:], [128, 2048])
        fence("b1")
        dump("xlnT1", XT[:], [128, CH * D])
        dump("phiq1", PHIQ[:], [128, CH * D])
        dump("phik1", PHIK[:], [128, CH * D])
        dump("kv1", KV[:], [128, CH * D])
        dump("tv1", TV1[0:1, :], [1, 2 * D])
        sp_evs = [s.dma("sync", spq_d[:, :], PHIQ[:], deps=b1),
                  s.dma("sync", spk_d[:, :], PHIK[:], deps=b1),
                  s.dma("sync", spkv_d[:, :], KV[:], deps=b1)]

        fence("spill")
        phase_A(1, XT)
        fence("a2")
        xt2_ready = s.barrier(("act",))[0]
        b2 = phase_B(1, XT, xt2_ready,
                     lambda base, fs: (CARKV if base == 0 else CARK)
                     [0:1, 512 * fs:512 * fs + 512], sp_evs)

        fence("b2")
        p1 = s.op("dve", lambda E: E.tensor_tensor(
            out=CARKV[0:1, :], in0=CARKV[0:1, :], in1=TV1[0:1, 0:D],
            op=OP.add), deps=b2 + b1, sig=True)
        p2 = s.op("dve", lambda E: E.tensor_tensor(
            out=CARK[0:1, :], in0=CARK[0:1, :], in1=TV1[0:1, D:2 * D],
            op=OP.add), deps=b2 + b1, sig=True)
        p3 = s.op("act", lambda E: E.activation(
            out=CARKV[0:1, :], in_=CARKV[0:1, :], func=AF.Copy,
            scale=FLAGE[0:1, 0:1]), deps=[p1, c_fe], sig=True)
        p4 = s.op("act", lambda E: E.activation(
            out=CARK[0:1, :], in_=CARK[0:1, :], func=AF.Copy,
            scale=FLAGE[0:1, 0:1]), deps=[p2], sig=True)
        di1 = s.dma("sync", spi_d[0:1, 0:D], CARKV[0:1, :], deps=[p3])
        di2 = s.dma("sync", spi_d[0:1, D:2 * D], CARK[0:1, :], deps=[p4])
        ccp = s.cc(lambda E: E.collective_compute(
            "AllReduce", OP.add,
            replica_groups=[[2 * i, 2 * i + 1] for i in range(4)],
            ins=[spi_d[:, :]], outs=[spo_d[:, :]]), deps=[di1, di2])
        do1 = s.dma("sync", CARKV[0:1, :], spo_d[0:1, 0:D], deps=[ccp])
        do2 = s.dma("sync", CARK[0:1, :], spo_d[0:1, D:2 * D], deps=[ccp])
        ci1, ci2 = do1, do2

        fence("cc1")
        dump("carkv2", CARKV[0:1, :], [1, D])
        dump("cark2", CARK[0:1, :], [1, D])
        r2 = phase_C(1, b2, [ci1, ci2], [
            (FLAGOR[:], FLAGO[0:1, 0:1],
             lambda nm, fs: (CARKV if nm == "kv" else CARK)
             [0:1, 512 * fs:512 * fs + 512]),
            (ONES1[:], ONES1[0:1, 0:1],
             lambda nm, fs: TV1[0:1, (0 if nm == "kv" else D) + 512 * fs:
                                (0 if nm == "kv" else D) + 512 * fs + 512]),
        ])
        fence("c2")
        c2_bar = s.barrier(("act", "pe", "dve", "pool"))
        dump("r2", PHIQ[:], [128, CH * D])
        d2 = phase_D(1, XT, r2)

        fence("d2")
        rl_evs = [s.dma("sync", PHIQ[:], spq_d[:, :],
                        deps=c2_bar + s.barrier(("act",))),
                  s.dma("sync", PHIK[:], spk_d[:, :], deps=c2_bar),
                  s.dma("sync", KV[:], spkv_d[:, :], deps=c2_bar)]
        do3 = s.dma("sync", CARKV[0:1, :], spo_d[0:1, 0:D], deps=c2_bar)
        do4 = s.dma("sync", CARK[0:1, :], spo_d[0:1, D:2 * D], deps=c2_bar)
        cj1, cj2 = do3, do4

        fence("rl")
        dump("rlq", PHIQ[:], [128, CH * D])
        dump("rlkv", KV[:], [128, CH * D])
        dump("rlk", PHIK[:], [128, CH * D])
        dump("carkv1", CARKV[0:1, :], [1, D])
        r1 = phase_C(0, rl_evs, [cj1, cj2], [
            (FLAGOR[:], FLAGO[0:1, 0:1],
             lambda nm, fs: (CARKV if nm == "kv" else CARK)
             [0:1, 512 * fs:512 * fs + 512]),
        ])
        fence("c1")
        dump("r1", PHIQ[:], [128, CH * D])
        d1 = phase_D(0, XT, r1)
        car_last = s.barrier(("pe", "act"))
        XT_cm.__exit__(None, None, None)
        CARK_cm.__exit__(None, None, None)
        CARKV_cm.__exit__(None, None, None)
        TV1_cm.__exit__(None, None, None)
        BIASR_cm = nc.sbuf_tensor("BIASR", [1, D], F32)
        BIASR = BIASR_cm.__enter__()
        c_br = s.dma("sync", BIASR[:], bias_d[:, :], deps=car_last)

        fence("d1")
        e_bar = s.barrier(("dve", "act", "pe", "pool"))
        if "xmid" in DBG:
            dd = dbg_decl("xmid", [TOK, D])
            s.dma("sync", dd[:, :], xmid_d[:, :], deps=e_bar + d1 + d2)
        am_evs = []
        for mc in range(2 * CH):
            row0 = mc * 128
            ld = s.dma("sync", YB[:], xmid_d[row0:row0 + 128, :],
                       deps=yb_free[0] + e_bar + d1 + d2)
            iv = ln_stats(YB[:], M2[:, mc:mc + 1], I2[:, mc:mc + 1], [ld])
            ue = s.op("dve", lambda E, mc=mc: E.tensor_scalar(
                out=SL0, in0=YB[:], scalar1=M2[:, mc:mc + 1],
                scalar2=I2[:, mc:mc + 1], op0=OP.subtract, op1=OP.mult),
                deps=[iv], sig=True)
            yb_free[0] = [ue]
            am = s.op("dve", lambda E, mc=mc: E.tensor_reduce(
                out=AMX[:, mc:mc + 1], in_=SL0, axis=AX.X, op=OP.max,
                apply_absolute_value=True), deps=[ue], sig=True)
            am_evs.append(am)
        amx_ev = s.op("dve", lambda E: E.tensor_reduce(
            out=AMX[:, 0:1], in_=AMX[:, 0:8], axis=AX.X, op=OP.max,
            apply_absolute_value=True), deps=am_evs, sig=True)
        cp1 = s.op("pool", lambda E: E.tensor_reduce(
            out=SC[0:1, 0:1], in_=AMX[:, 0:1], axis=AX.C, op=OP.max),
            deps=[amx_ev], sig=True)
        cp2 = s.op("pool", lambda E: E.tensor_reduce(
            out=SC[0:1, 1:2], in_=AMW[:, 0:1], axis=AX.C, op=OP.max),
            deps=[amw_ev], sig=True)
        cb = s.op("dve", lambda E: E.tensor_tensor(
            out=SC[0:1, 2:3], in0=SC[0:1, 0:1], in1=SC[0:1, 1:2], op=OP.max),
            deps=[cp1, cp2], sig=True)
        bc1 = s.op("pool", lambda E: E.tensor_copy(
            CX[0:1, 0:8], SC[0:1, 0:1].broadcast_to((1, 8))), deps=[cp1],
            sig=True)
        bc2 = s.op("pool", lambda E: E.tensor_copy(
            CX[0:1, 8:16], SC[0:1, 2:3].broadcast_to((1, 8))), deps=[cb],
            sig=True)
        dci = s.dma("sync", cxi_d[:, :], CX[0:1, :], deps=[bc1, bc2])
        ccx = s.cc(lambda E: E.collective_compute(
            "AllReduce", OP.max, replica_groups=[list(range(NC))],
            ins=[cxi_d[:, :]], outs=[cxo_d[:, :]]), deps=[dci])
        dco = s.dma("sync", CX2[0:1, :], cxo_d[:, :], deps=[ccx])
        s1 = s.op("dve", lambda E: E.reciprocal(
            out=SC[0:1, 3:4], in_=CX2[0:1, 0:1]), deps=[dco], sig=True)
        s2 = s.op("dve", lambda E: E.tensor_scalar(
            out=SC[0:1, 4:5], in0=SC[0:1, 3:4], scalar1=K358, scalar2=None,
            op0=OP.mult), deps=[s1], sig=True)
        s3 = s.op("dve", lambda E: E.reciprocal(
            out=SC[0:1, 5:6], in_=CX2[0:1, 8:9]), deps=[dco], sig=True)
        s4 = s.op("dve", lambda E: E.tensor_scalar(
            out=SC[0:1, 6:7], in0=SC[0:1, 5:6], scalar1=K358, scalar2=None,
            op0=OP.mult), deps=[s3], sig=True)
        s5 = s.op("dve", lambda E: E.tensor_tensor(
            out=SC[0:1, 7:8], in0=CX2[0:1, 0:1], in1=CX2[0:1, 8:9],
            op=OP.mult), deps=[dco], sig=True)
        s6 = s.op("dve", lambda E: E.tensor_scalar(
            out=SC[0:1, 8:9], in0=SC[0:1, 7:8], scalar1=1.0 / (K358 * K358),
            scalar2=None, op0=OP.mult), deps=[s5], sig=True)
        s7 = s.op("dve", lambda E: E.reciprocal(
            out=SC[0:1, 9:10], in_=SC[0:1, 8:9]), deps=[s6], sig=True)
        bsc = s.op("act", lambda E: E.activation(
            out=BIASR[0:1, :], in_=BIASR[0:1, :], func=AF.Copy,
            scale=SC[0:1, 9:10]), deps=[s7, c_br], sig=True)
        bevs = []
        for (col, src_ev, scol) in ((1, s2, 4), (2, s4, 6), (3, s6, 8)):
            mm = s.op("pe", lambda E, scol=scol: E.matmul(
                PB[5][:, 0:1], ONES1[:], SC[0:1, scol:scol + 1],
                start=True, stop=True),
                deps=[src_ev, c_o1] + bank_free[5], sig=True)
            cp = s.op("act", lambda E, col=col: E.copy(
                out=B128[:, col:col + 1], in_=PB[5][:, 0:1]), deps=[mm],
                sig=True)
            bank_free[5] = [cp]
            bevs.append(cp)

        def emit_quant(src_ap, out_ap, scale_col, sA, sB, sC, sD, deps):
            e1 = s.op("act", lambda E: E.activation(
                out=sA, in_=src_ap, func=AF.Copy,
                scale=B128[:, scale_col:scale_col + 1]), deps=deps, sig=True)
            e2 = s.op("dve", lambda E: E.scalar_tensor_tensor(
                out=sC, in0=sA, scalar=PHI4_32, in1=sA,
                op0=OP.mult, op1=OP.subtract), deps=[e1], sig=True)
            e3 = s.op("dve", lambda E: E.scalar_tensor_tensor(
                out=sB, in0=sA, scalar=PHI4_32, in1=sC,
                op0=OP.mult, op1=OP.subtract), deps=[e2], sig=True)
            e4 = s.op("pool", lambda E: E.tensor_scalar(
                out=sC, in0=sA, scalar1=PHI4_64, scalar2=None, op0=OP.mult),
                deps=[e3], sig=True)
            e5 = s.op("pool", lambda E: E.tensor_tensor(
                out=sD, in0=sC, in1=sA, op=OP.subtract), deps=[e4], sig=True)
            e6 = s.op("pool", lambda E: E.tensor_tensor(
                out=sC, in0=sC, in1=sD, op=OP.subtract), deps=[e5], sig=True)
            e7 = s.op("pool", lambda E: E.tensor_tensor(
                out=sC, in0=sC, in1=sB, op=OP.subtract), deps=[e6], sig=True)
            return s.op("dve", lambda E: E.scalar_tensor_tensor(
                out=out_ap, in0=sC, scalar=B128[:, 0:1], in1=sB,
                op0=OP.mult, op1=OP.add), deps=[e7, ind_ev], sig=True)

        fence("e")
        dump("sc", SC[0:1, :], [1, 16])
        dump("cx2", CX2[0:1, :], [1, 16])
        dump("b128", B128[:, :], [128, 4])
        dump("m2", M2[:, :], [128, 2 * CH])
        dump("i2", I2[:, :], [128, 2 * CH])
        QX_cm = nc.sbuf_tensor("QX", [128, 16 * TOK], BF16)
        QX = QX_cm.__enter__()
        QW_cm = nc.sbuf_tensor("QW", [128, 4096], BF16)
        QW = QW_cm.__enter__()

        fence("scales")
        qx_evs = []
        for mc in range(2 * CH):
            row0 = mc * 128
            ld = s.dma("sync", YB[:], xmid_d[row0:row0 + 128, :],
                       deps=yb_free[0] + d1 + d2)
            ue = s.op("dve", lambda E, mc=mc: E.tensor_scalar(
                out=SL0, in0=YB[:], scalar1=M2[:, mc:mc + 1],
                scalar2=I2[:, mc:mc + 1], op0=OP.subtract, op1=OP.mult),
                deps=[ld] + qx_evs[-4:], sig=True)
            ut_evs = transpose_to(
                lambda j: SL0[:, 128 * j:128 * j + 128],
                lambda jb: SL0[:, 512 * jb:512 * jb + 512],
                [ue])
            last_q = []
            for qq in range(4):
                srcq = SL0[:, 512 * qq:512 * qq + 512]
                dst = bass.AP(QX, 4096 * qq + 128 * mc,
                              [[16 * TOK, 128], [TOK, 4], [1, 128]])
                e7 = emit_quant(srcq, dst, 1,
                                YB[:, 0:512], YB[:, 512:1024],
                                YB[:, 1024:1536], YB[:, 1536:2048],
                                [ut_evs[qq], bevs[0]] + last_q)
                last_q = [e7]
            qx_evs += last_q
            yb_free[0] = last_q
        fence("qx")
        qx_done = s.barrier(("dve", "pool", "act"))
        dump("qx", QX[:], [128, 16 * TOK], BF16)

        for os in range(4):
            fence(f"f{os}")
            fill_evs = [[] for _ in range(2 * CH)]
            for kcr in range(2):
                wb = WB[kcr]
                src = bass.AP(
                    wffn_d, os * 512 + kcr * 8 * 128 * D,
                    [[D, 128], [128 * D, 8], [1, 512]])
                dd = s.dma("sync",
                           wb[:].rearrange("p (k c) -> p k c", k=8), src,
                           deps=wb_free[kcr] + (qx_done if os == 0 else []))
                qdone = []
                for qq in range(8):
                    srcq = wb[:, 512 * qq:512 * qq + 512]
                    w7 = emit_quant(
                        srcq, QW[:, 512 * qq:512 * qq + 512], 2,
                        SL0[:, 0:512], SL0[:, 512:1024],
                        SL0[:, 1024:1536], SL0[:, 1536:2048],
                        [dd, bevs[1]] + qdone[-1:] + qx_done
                        + (qw_guard[0] if qq == 0 else []))
                    qdone.append(w7)
                mm_last = [None] * (2 * CH)
                for kc8 in range(8):
                    kc = kcr * 8 + kc8
                    for mc in range(2 * CH):
                        st = (kc == 0)
                        deps = [qdone[kc8]]
                        if st:
                            deps += bank_free[mc]
                        mm_last[mc] = s.op(
                            "pe", lambda E, mc=mc, kc=kc, kc8=kc8:
                            E.matmul(
                                PB[mc][:],
                                QX[:, TOK * kc + 128 * mc:
                                   TOK * kc + 128 * mc + 128],
                                QW[:, 512 * kc8:512 * kc8 + 512],
                                start=(kc == 0), stop=False),
                            deps=deps,
                            sig=((kc8 == 7 and mc == 2 * CH - 1)
                                 or kc == 15))
                wb_free[kcr] = [mm_last[2 * CH - 1]]
                for mc in range(2 * CH):
                    fill_evs[mc].append(mm_last[mc])
                qw_guard[0] = [mm_last[2 * CH - 1]]
            for mc in range(2 * CH):
                bm = s.op("pe", lambda E, mc=mc, os=os: E.matmul(
                    PB[mc][:], ONES1[:],
                    BIASR[0:1, 512 * os:512 * os + 512],
                    start=False, stop=True),
                    deps=[bsc] + fill_evs[mc], sig=True)
                fill_evs[mc].append(bm)
            for mc in range(2 * CH):
                row0 = mc * 128
                xld = s.dma("sync", YB[:, 512 * os:512 * os + 512],
                            xmid_d[row0:row0 + 128, 512 * os:512 * os + 512],
                            deps=yb_free[0] + d1 + d2)
                ev = s.op("dve", lambda E, mc=mc, os=os:
                          E.scalar_tensor_tensor(
                              out=YB[:, 512 * os:512 * os + 512],
                              in0=PB[mc][:], scalar=B128[:, 3:4],
                              in1=YB[:, 512 * os:512 * os + 512],
                              op0=OP.mult, op1=OP.add),
                          deps=[xld, bevs[2]] + fill_evs[mc][-1:], sig=True)
                bank_free[mc] = [ev]
                sto = s.dma("sync",
                            y_d[row0:row0 + 128, 512 * os:512 * os + 512],
                            YB[:, 512 * os:512 * os + 512], deps=[ev])
                yb_free[0] = [sto]

        QW_cm.__exit__(None, None, None)
        QX_cm.__exit__(None, None, None)
        BIASR_cm.__exit__(None, None, None)

        last_out = ("dpool", s.cnt["dpool"])
        s.op("pool", lambda E: E.engine_nop(), deps=[last_out])

        with nc.Block() as block:
            s.emit(block, semh)
    return nc


_NC_CACHE = {}


def kernel(L0_attn_x, L0_attn_W_qkv, L0_attn_W_out, L0_ffn_weight,
           L0_ffn_bias):
    x = np.ascontiguousarray(
        np.asarray(L0_attn_x, np.float32)).reshape(NC * TOK, D)
    wqkvT = np.ascontiguousarray(np.asarray(L0_attn_W_qkv, np.float32).T)
    woutT = np.ascontiguousarray(np.asarray(L0_attn_W_out, np.float32).T)
    wffnT = np.ascontiguousarray(np.asarray(L0_ffn_weight, np.float32).T)
    biasr = np.ascontiguousarray(
        np.asarray(L0_ffn_bias, np.float32)).reshape(1, D)
    tri = np.triu(np.ones((128, 128), np.float32))
    ident = np.eye(128, dtype=np.float32)
    ones1 = np.ones((1, 128), np.float32)
    onesc = np.ones((128, 1), np.float32)

    if "nc" not in _NC_CACHE:
        _NC_CACHE["nc"] = build_program()
    nc = _NC_CACHE["nc"]
    in_maps = []
    for c in range(NC):
        in_maps.append({
            "x": x[c * TOK:(c + 1) * TOK],
            "wqkvT": wqkvT, "woutT": woutT, "wffnT": wffnT, "biasr": biasr,
            "tri": tri, "ident": ident, "ones1": ones1, "onesc": onesc,
            "flago": np.full((1, 1), float(c % 2), np.float32),
            "flagor": np.full((1, 128), float(c % 2), np.float32),
            "flage": np.full((1, 1), float(1 - c % 2), np.float32),
        })
    res = run_bass_kernel_spmd(nc, in_maps, core_ids=list(range(NC)))
    y = np.concatenate([np.asarray(r["y"]) for r in res.results], axis=0)
    return y.reshape(4, 2048, 2048)


# version 06
# speedup vs baseline: 1.1986x; 1.1741x over previous
import numpy as np
import concourse.bass as bass
import concourse.mybir as mybir
from concourse.bass_utils import run_bass_kernel_spmd

F32 = mybir.dt.float32
BF16 = mybir.dt.bfloat16
AF = mybir.ActivationFunctionType
OP = mybir.AluOpType
AX = mybir.AxisListType

D = 2048
TOK = 1024
G = 2
GT = TOK // G
CH = GT // 128
NC = 8

PHI4_32 = float(2 ** 20 + 1)
PHI4_64 = float(2 ** 49 + 1)
K358 = 448.0 * 0.8
EPS_LN = 1e-5
EPS_ATTN = 1e-9
CSC = 128.0 ** -0.5


class Sched:
    ENG2BLOCK = {"pe": "tensor", "dve": "vector", "act": "scalar",
                 "pool": "gpsimd", "sync": "sync"}

    def __init__(self):
        self.q = {e: [] for e in self.ENG2BLOCK}
        self.cnt = {"pe": 0, "dve": 0, "act": 0, "pool": 0,
                    "dsync": 0, "dpool": 0, "cc": 0}
        self.gate = []

    def op(self, eng, fn, deps=(), sig=False, drain=False):
        deps = [d for d in deps if d is not None] + self.gate
        ev = None
        if sig:
            self.cnt[eng] += 1
            ev = (eng, self.cnt[eng])
        self.q[eng].append([deps, fn, ("c", ev) if ev else None,
                            drain and sig])
        return ev

    def dma(self, eng, out, in_, deps=()):
        sem = "dsync" if eng == "sync" else "dpool"
        deps = [d for d in deps if d is not None] + self.gate
        self.cnt[sem] += 16
        ev = (sem, self.cnt[sem])
        self.q[eng].append(
            [deps, (lambda E, o=out, i=in_: E.dma_start(out=o, in_=i)),
             ("d", ev), False])
        return ev

    def cc(self, fn, deps=()):
        deps = [d for d in deps if d is not None]
        self.cnt["cc"] += 1
        ev = ("cc", self.cnt["cc"])
        self.q["pool"].append([deps, fn, ("c", ev), False])
        return ev

    def barrier(self, engines=("pe", "dve", "act", "pool")):
        evs = []
        for e in engines:
            if not self.q[e]:
                continue
            last = self.q[e][-1]
            if last[2] is None:
                self.cnt[e] += 1
                ev = (e, self.cnt[e])
                last[2] = ("c", ev)
                last[3] = True
            else:
                ev = last[2][1]
            evs.append(ev)
        return evs

    def emit(self, block, semh):
        for eng, bname in self.ENG2BLOCK.items():
            items = self.q[eng]
            own = eng

            def body(E, items=items, own=own):
                waited = {}
                for deps, fn, sig, drn in items:
                    for (sname, v) in deps:
                        if waited.get(sname, 0) < v:
                            E.wait_ge(semh[sname], v)
                            waited[sname] = v
                    ins = fn(E)
                    if sig is not None:
                        kind, (sname, v) = sig
                        if (drn or own == "pe") and kind == "c":
                            ins = E.drain()
                        ins.then_inc(semh[sname], 16 if kind == "d" else 1)

            getattr(block, bname)(body)


DBG = set()
FENCES = set()


def build_program():
    nc = bass.Bass("TRN2", target_bir_lowering=False,
                   dynamic_dma_scratch_size=4096)
    dbg_outs = {}

    def dbg_decl(name, shape, dt=F32):
        dbg_outs[name] = nc.dram_tensor("dbg_" + name, shape, dt,
                                        kind="ExternalOutput")
        return dbg_outs[name]

    x_d = nc.dram_tensor("x", [TOK, D], F32, kind="ExternalInput")
    wqkv_d = nc.dram_tensor("wqkvT", [D, 3 * D], F32, kind="ExternalInput")
    wout_d = nc.dram_tensor("woutT", [D, D], F32, kind="ExternalInput")
    wffn_d = nc.dram_tensor("wffnT", [D, D], F32, kind="ExternalInput")
    bias_d = nc.dram_tensor("biasr", [1, D], F32, kind="ExternalInput")
    tri_d = nc.dram_tensor("tri", [128, 128], F32, kind="ExternalInput")
    id_d = nc.dram_tensor("ident", [128, 128], F32, kind="ExternalInput")
    ones1_d = nc.dram_tensor("ones1", [1, 128], F32, kind="ExternalInput")
    onesc_d = nc.dram_tensor("onesc", [128, 1], F32, kind="ExternalInput")
    flago_d = nc.dram_tensor("flago", [1, 1], F32, kind="ExternalInput")
    flagor_d = nc.dram_tensor("flagor", [1, 128], F32, kind="ExternalInput")
    flage_d = nc.dram_tensor("flage", [1, 1], F32, kind="ExternalInput")
    y_d = nc.dram_tensor("y", [TOK, D], F32, kind="ExternalOutput")
    xmid_d = nc.dram_tensor("xmid_i", [TOK, D], F32)
    spq_d = nc.dram_tensor("spq_i", [128, CH * D], F32)
    spk_d = nc.dram_tensor("spk_i", [128, CH * D], F32)
    spkv_d = nc.dram_tensor("spkv_i", [128, CH * D], F32)
    spi_d = nc.dram_tensor("spi_i", [1, 2 * D], F32)
    spo_d = nc.dram_tensor("spo_i", [1, 2 * D], F32)
    cxi_d = nc.dram_tensor("cxi_i", [1, 16], F32)
    cxo_d = nc.dram_tensor("cxo_i", [1, 16], F32, addr_space="Shared")

    from contextlib import ExitStack
    s = Sched()
    with ExitStack() as es:
        e = es.enter_context
        PHIQ = e(nc.sbuf_tensor("PHIQ", [128, CH * D], F32))
        KV = e(nc.sbuf_tensor("KV", [128, CH * D], F32))
        PHIK = e(nc.sbuf_tensor("PHIK", [128, CH * D], F32))
        WB = [e(nc.sbuf_tensor(f"WB{i}", [128, 4096], F32)) for i in range(2)]
        QSCR = e(nc.sbuf_tensor("QSCR", [128, 4096], F32))
        YB = e(nc.sbuf_tensor("YB", [128, D], F32))
        TRI = e(nc.sbuf_tensor("TRI", [128, 128], F32))
        IDN = e(nc.sbuf_tensor("IDN", [128, 128], F32))
        ONES1 = e(nc.sbuf_tensor("ONES1", [1, 128], F32))
        ONESC = e(nc.sbuf_tensor("ONESC", [128, 1], F32))
        FLAGO = e(nc.sbuf_tensor("FLAGO", [1, 1], F32))
        FLAGOR = e(nc.sbuf_tensor("FLAGOR", [1, 128], F32))
        FLAGE = e(nc.sbuf_tensor("FLAGE", [1, 1], F32))
        CX = e(nc.sbuf_tensor("CX", [1, 16], F32))
        CX2 = e(nc.sbuf_tensor("CX2", [1, 16], F32))
        SC = e(nc.sbuf_tensor("SC", [1, 16], F32))
        ST = e(nc.sbuf_tensor("ST", [128, 8], F32))
        M1 = e(nc.sbuf_tensor("M1", [128, CH], F32))
        I1 = e(nc.sbuf_tensor("I1", [128, CH], F32))
        M2 = e(nc.sbuf_tensor("M2", [128, 2 * CH], F32))
        I2 = e(nc.sbuf_tensor("I2", [128, 2 * CH], F32))
        AMX = e(nc.sbuf_tensor("AMX", [128, 8], F32))
        AMW = e(nc.sbuf_tensor("AMW", [128, 16], F32))
        B128 = e(nc.sbuf_tensor("B128", [128, 4], F32))
        PB = [e(nc.psum_tensor(f"PB{i}", [128, 512], F32)) for i in range(8)]
        semh = {n: e(nc.semaphore(n)) for n in
                ("pe", "dve", "act", "pool", "dsync", "dpool", "cc")}

        SL0 = QSCR[:, 0:2048]
        SL1 = QSCR[:, 2048:4096]

        bank_free = [[] for _ in range(8)]
        wb_free = [[], []]
        yb_free = [[]]

        def fence(name):
            if name != "rl" and "ALL" not in FENCES and name not in FENCES:
                return
            evs = s.barrier(("pe", "dve", "act", "pool"))
            evs += [("dsync", s.cnt["dsync"]), ("dpool", s.cnt["dpool"])]
            if s.cnt["cc"]:
                evs.append(("cc", s.cnt["cc"]))
            s.gate = evs

        def dump(name, src_ap, shape, dt=F32):
            if name not in DBG:
                return
            dd = dbg_decl(name, shape, dt)
            ev = s.dma("sync", dd[tuple(slice(0, x) for x in shape)], src_ap,
                       deps=s.barrier(("pe", "dve", "act", "pool")))
            s.gate = [ev]

        c_tri = s.dma("sync", TRI[:], tri_d[:, :])
        c_id = s.dma("sync", IDN[:], id_d[:, :])
        c_o1 = s.dma("sync", ONES1[:], ones1_d[:, :])
        c_oc = s.dma("sync", ONESC[:], onesc_d[:, :])
        c_fo = s.dma("sync", FLAGO[:], flago_d[:, :])
        c_for = s.dma("sync", FLAGOR[:], flagor_d[:, :])
        c_fe = s.dma("sync", FLAGE[:], flage_d[:, :])

        iv0 = s.op("dve", lambda E: E.memset(B128[:, 0:1], 1.0), sig=True)
        iv1 = s.op("dve", lambda E: E.tensor_scalar(
            out=B128[:, 0:1], in0=B128[:, 0:1], scalar1=float(2.0 ** -30),
            scalar2=None, op0=OP.add), deps=[iv0], sig=True)
        ind_ev = s.op("dve", lambda E: E.tensor_scalar(
            out=B128[:, 0:1], in0=B128[:, 0:1], scalar1=1.0,
            scalar2=float(2.0 ** 30), op0=OP.subtract, op1=OP.mult),
            deps=[iv1], sig=True)

        amw_parts = []
        for i in range(8):
            src = bass.AP(wffn_d, i * 256, [[D, 128], [128 * D, 16], [1, 256]])
            dst = WB[i % 2][:].rearrange("p (k c) -> p k c", k=16)
            ld = s.dma("sync", dst, src, deps=wb_free[i % 2])
            rd = s.op("dve", lambda E, i=i, b=i % 2: E.tensor_reduce(
                out=AMW[:, i:i + 1], in_=WB[b][:], axis=AX.X, op=OP.max,
                apply_absolute_value=True), deps=[ld], sig=True)
            wb_free[i % 2] = [rd]
            amw_parts.append(rd)
        amw_ev = s.op("dve", lambda E: E.tensor_reduce(
            out=AMW[:, 0:1], in_=AMW[:, 0:8], axis=AX.X, op=OP.max,
            apply_absolute_value=True), deps=amw_parts, sig=True, drain=True)

        def ln_stats(chunk_src, mean_ap, inv_ap, deps):
            r1 = s.op("dve", lambda E: E.tensor_reduce(
                out=ST[:, 0:1], in_=chunk_src, axis=AX.X, op=OP.add),
                deps=deps, sig=True)
            sq = s.op("act", lambda E: E.activation(
                out=SL1, in_=chunk_src, func=AF.Square), deps=deps, sig=True,
                drain=True)
            r2 = s.op("dve", lambda E: E.tensor_reduce(
                out=ST[:, 1:2], in_=SL1, axis=AX.X, op=OP.add),
                deps=[sq], sig=True)
            g1 = s.op("dve", lambda E: E.tensor_scalar(
                out=mean_ap, in0=ST[:, 0:1], scalar1=1.0 / D, scalar2=None,
                op0=OP.mult), deps=[r1], sig=True)
            g2 = s.op("dve", lambda E: E.tensor_scalar(
                out=ST[:, 2:3], in0=ST[:, 1:2], scalar1=1.0 / D, scalar2=None,
                op0=OP.mult), deps=[r2], sig=True)
            g3 = s.op("dve", lambda E: E.tensor_tensor(
                out=ST[:, 3:4], in0=mean_ap, in1=mean_ap, op=OP.mult),
                deps=[g1], sig=True)
            g4 = s.op("dve", lambda E: E.tensor_tensor(
                out=ST[:, 4:5], in0=ST[:, 2:3], in1=ST[:, 3:4],
                op=OP.subtract), deps=[g2, g3], sig=True)
            ve = s.op("dve", lambda E: E.tensor_scalar(
                out=ST[:, 5:6], in0=ST[:, 4:5], scalar1=EPS_LN, scalar2=None,
                op0=OP.add), deps=[g4], sig=True)
            sd = s.op("act", lambda E: E.activation(
                out=ST[:, 6:7], in_=ST[:, 5:6], func=AF.Sqrt), deps=[ve],
                sig=True, drain=True)
            return s.op("dve", lambda E: E.reciprocal(
                out=inv_ap, in_=ST[:, 6:7]), deps=[sd], sig=True)

        def transpose_to(src_tile_fn, dst_fn, deps):
            evs = []
            for jb in range(4):
                bidx = 6 + (jb % 2)
                bank = PB[bidx]
                last = None
                for t in range(4):
                    j = 4 * jb + t
                    dd = (deps + bank_free[bidx] + [c_id]) if t == 0 else []
                    last = s.op("pe", lambda E, j=j, t=t, b=bank: E.transpose(
                        b[:, 128 * t:128 * t + 128], src_tile_fn(j), IDN[:]),
                        deps=dd, sig=(t == 3))
                ev = s.op("act", lambda E, jb=jb, b=bank: E.copy(
                    out=dst_fn(jb), in_=b[:, 0:512]), deps=[last], sig=True)
                bank_free[bidx] = [ev]
                evs.append(ev)
            return evs

        def phase_A(g, XT):
            tp_evs = []
            for mg in range(CH):
                row0 = g * GT + mg * 128
                ld = s.dma("sync", YB[:], x_d[row0:row0 + 128, :],
                           deps=yb_free[0])
                iv = ln_stats(YB[:], M1[:, mg:mg + 1], I1[:, mg:mg + 1], [ld])
                ap = s.op("dve", lambda E, mg=mg: E.tensor_scalar(
                    out=SL0, in0=YB[:], scalar1=M1[:, mg:mg + 1],
                    scalar2=I1[:, mg:mg + 1], op0=OP.subtract, op1=OP.mult),
                    deps=[iv] + tp_evs[-4:], sig=True, drain=True)
                yb_free[0] = [ap]
                tp_evs = transpose_to(
                    lambda j: SL0[:, 128 * j:128 * j + 128],
                    lambda jb, mg=mg: bass.AP(
                        XT, 2048 * jb + 128 * mg,
                        [[CH * D, 128], [512, 4], [1, 128]]),
                    [ap])
            return tp_evs

        def phase_B(g, XT, xt_ready, totg_fn, extra):
            for fs in range(4):
                phk_evs = [None] * CH
                kv_evs = [None] * CH
                for tgt in range(3):
                    col0 = tgt * D + fs * 512
                    mm_last = [None] * CH
                    for kcr in range(2):
                        wb = WB[kcr]
                        src = bass.AP(
                            wqkv_d, col0 + kcr * 8 * 128 * 3 * D,
                            [[3 * D, 128], [128 * 3 * D, 8], [1, 512]])
                        dd = s.dma("sync",
                                   wb[:].rearrange("p (k c) -> p k c", k=8),
                                   src, deps=wb_free[kcr])
                        for kc8 in range(8):
                            kc = kcr * 8 + kc8
                            for mg in range(CH):
                                st = (kc == 0)
                                deps = [dd]
                                if st:
                                    deps += bank_free[mg] + [xt_ready]
                                mm_last[mg] = s.op(
                                    "pe",
                                    lambda E, mg=mg, kc=kc, kc8=kc8,
                                    wb=wb, XT=XT: E.matmul(
                                        PB[mg][:],
                                        XT[:, 512 * kc + 128 * mg:
                                           512 * kc + 128 * mg + 128],
                                        wb[:, 512 * kc8:512 * kc8 + 512],
                                        start=(kc == 0), stop=(kc == 15)),
                                    deps=deps,
                                    sig=((kc8 == 7 and mg == CH - 1)
                                         or kc == 15))
                        wb_free[kcr] = [mm_last[CH - 1]]
                    for mg in range(CH):
                        bank = PB[mg]
                        dsl = slice(D * mg + 512 * fs, D * mg + 512 * fs + 512)
                        if tgt < 2:
                            out = PHIQ if tgt == 0 else PHIK
                            t1 = s.op("dve", lambda E, bank=bank: E.tensor_scalar(
                                out=SL0[:, 0:512], in0=bank[:], scalar1=0.0,
                                scalar2=None, op0=OP.min),
                                deps=[mm_last[mg]], sig=True)
                            ex = s.op("act", lambda E: E.activation(
                                out=SL0[:, 512:1024], in_=SL0[:, 0:512],
                                func=AF.Exp), deps=[t1], sig=True, drain=True)
                            rl = s.op("act", lambda E, bank=bank: E.activation(
                                out=SL0[:, 1024:1536], in_=bank[:],
                                func=AF.Relu), deps=[mm_last[mg]], sig=True,
                                drain=True)
                            ph = s.op("dve",
                                      lambda E, out=out, dsl=dsl: E.tensor_tensor(
                                          out=out[:, dsl], in0=SL0[:, 512:1024],
                                          in1=SL0[:, 1024:1536], op=OP.add),
                                      deps=[ex, rl] + extra, sig=True,
                                      drain=True)
                            bank_free[mg] = [t1, rl]
                            if tgt == 1:
                                phk_evs[mg] = ph
                        else:
                            kve = s.op("dve",
                                       lambda E, dsl=dsl, bank=bank:
                                       E.tensor_tensor(
                                           out=KV[:, dsl], in0=PHIK[:, dsl],
                                           in1=bank[:], op=OP.mult),
                                       deps=[mm_last[mg], phk_evs[mg]] + extra,
                                       sig=True, drain=True)
                            bank_free[mg] = [kve]
                            kv_evs[mg] = kve
                for (tens, base, sevs) in ((KV, 0, kv_evs), (PHIK, D, phk_evs)):
                    mms = []
                    for mg in range(CH):
                        mms.append(s.op(
                            "pe", lambda E, tens=tens, mg=mg, fs=fs: E.matmul(
                                PB[4][0:1, 0:512], ONESC[:],
                                tens[:, D * mg + 512 * fs:
                                     D * mg + 512 * fs + 512],
                                start=(mg == 0), stop=(mg == CH - 1)),
                            deps=(sevs + bank_free[4] + [c_oc])
                            if mg == 0 else [], sig=(mg == CH - 1)))
                    tev = s.op("act", lambda E, base=base, fs=fs,
                               totg_fn=totg_fn: E.copy(
                                   out=totg_fn(base, fs), in_=PB[4][0:1, 0:512]),
                               deps=[mms[-1]], sig=True, drain=True)
                    bank_free[4] = [tev]
            return s.barrier(("dve", "act", "pe"))

        def phase_C(g, src_deps, car_init, car0):
            r_evs = []
            car_ev = {}
            for fs in range(4):
                car_ev[("kv", fs)] = list(car_init)
                car_ev[("k", fs)] = list(car_init)
            for mg in range(CH):
                for fs in range(4):
                    par = fs % 2
                    SLP = SL0 if par == 0 else YB
                    sl = slice(D * mg + 512 * fs, D * mg + 512 * fs + 512)
                    evd = {}
                    for nm, tens, car, bi in (
                            ("kv", KV, CARKV, 4 if par == 0 else 0),
                            ("k", PHIK, CARK, 5 if par == 0 else 1)):
                        bc = bi + 2
                        s.op("pe", lambda E, tens=tens, sl=sl, bi=bi:
                             E.matmul(PB[bi][:], TRI[:], tens[:, sl],
                                      start=True, stop=False),
                             deps=bank_free[bi] + src_deps + [c_tri])
                        if mg == 0:
                            terms = [(row, scl, rfn(nm, fs))
                                     for (row, scl, rfn) in car0]
                        else:
                            terms = [(ONES1[:], ONES1[0:1, 0:1],
                                      car[0:1, 512 * fs:512 * fs + 512])]
                        t2 = t3 = None
                        nt = len(terms)
                        for ti, (row, scl, rhs) in enumerate(terms):
                            t2 = s.op("pe", lambda E, row=row, rhs=rhs, bi=bi,
                                      last=(ti == nt - 1):
                                      E.matmul(PB[bi][:], row, rhs,
                                               start=False, stop=last),
                                      deps=car_ev[(nm, fs)] + [c_o1, c_for],
                                      sig=(ti == nt - 1))
                        s.op("pe", lambda E, tens=tens, sl=sl, bc=bc:
                             E.matmul(PB[bc][0:1, 0:512], ONESC[:],
                                      tens[:, sl], start=True, stop=False),
                             deps=bank_free[bc] + [c_oc])
                        for ti, (row, scl, rhs) in enumerate(terms):
                            t3 = s.op("pe", lambda E, scl=scl, rhs=rhs, bc=bc,
                                      last=(ti == nt - 1):
                                      E.matmul(PB[bc][0:1, 0:512], scl, rhs,
                                               start=False, stop=last),
                                      deps=car_ev[(nm, fs)],
                                      sig=(ti == nt - 1))
                        cx = s.op("act", lambda E, car=car, fs=fs, bc=bc:
                                  E.copy(out=car[0:1, 512 * fs:512 * fs + 512],
                                         in_=PB[bc][0:1, 0:512]),
                                  deps=[t3, t2], sig=True, drain=True)
                        bank_free[bc] = [cx]
                        car_ev[(nm, fs)] = [cx]
                        evd[nm] = t2
                        evd[nm + "x"] = cx
                    bkv, bk = (4, 5) if par == 0 else (0, 1)
                    num = s.op("dve", lambda E, sl=sl, bkv=bkv, SLP=SLP: E.tensor_tensor(
                        out=SLP[:, 0:512], in0=PHIQ[:, sl], in1=PB[bkv][:],
                        op=OP.mult), deps=[evd["kv"]], sig=True)
                    den = s.op("dve", lambda E, sl=sl, bk=bk, SLP=SLP: E.tensor_tensor(
                        out=SLP[:, 512:1024], in0=PHIQ[:, sl], in1=PB[bk][:],
                        op=OP.mult), deps=[evd["k"]], sig=True)
                    bank_free[bkv] = [num, evd["kvx"]]
                    bank_free[bk] = [den, evd["kx"]]
                    de = s.op("act", lambda E, SLP=SLP: E.activation(
                        out=SLP[:, 512:1024], in_=SLP[:, 512:1024],
                        func=AF.Copy, bias=EPS_ATTN, scale=1.0),
                        deps=[den], sig=True, drain=True)
                    rc = s.op("dve", lambda E, SLP=SLP: E.reciprocal(
                        out=SLP[:, 1024:1536], in_=SLP[:, 512:1024]),
                        deps=[de], sig=True)
                    rv = s.op("dve", lambda E, sl=sl, SLP=SLP: E.tensor_tensor(
                        out=PHIQ[:, sl], in0=SLP[:, 0:512],
                        in1=SLP[:, 1024:1536], op=OP.mult), deps=[rc],
                        sig=True, drain=True)
                    r_evs.append(rv)
            return r_evs

        def phase_D(g, RT, r_deps):
            tp = []
            for mg in range(CH):
                tp += transpose_to(
                    lambda j, mg=mg: PHIQ[:, D * mg + 128 * j:
                                          D * mg + 128 * j + 128],
                    lambda jb, mg=mg: bass.AP(
                        RT, 2048 * jb + 128 * mg,
                        [[CH * D, 128], [512, 4], [1, 128]]),
                    r_deps)
            rt_ready = s.barrier(("act",))
            st_evs = []
            for os in range(4):
                xld = [None] * CH
                for mg in range(CH):
                    row0 = g * GT + mg * 128
                    xld[mg] = s.dma(
                        "sync", SL1[:, 512 * mg:512 * mg + 512],
                        x_d[row0:row0 + 128, 512 * os:512 * os + 512],
                        deps=sl1_free[0])
                mm_last = [None] * CH
                for kcr in range(2):
                    wb = WB[kcr]
                    src = bass.AP(
                        wout_d, os * 512 + kcr * 8 * 128 * D,
                        [[D, 128], [128 * D, 8], [1, 512]])
                    dd = s.dma("sync",
                               wb[:].rearrange("p (k c) -> p k c", k=8),
                               src, deps=wb_free[kcr])
                    for kc8 in range(8):
                        kc = kcr * 8 + kc8
                        for mg in range(CH):
                            st = (kc == 0)
                            deps = [dd]
                            if st:
                                deps += bank_free[mg] + rt_ready
                            mm_last[mg] = s.op(
                                "pe", lambda E, mg=mg, kc=kc, kc8=kc8,
                                wb=wb, RT=RT: E.matmul(
                                    PB[mg][:],
                                    RT[:, 512 * kc + 128 * mg:
                                       512 * kc + 128 * mg + 128],
                                    wb[:, 512 * kc8:512 * kc8 + 512],
                                    start=(kc == 0), stop=(kc == 15)),
                                deps=deps,
                                sig=((kc8 == 7 and mg == CH - 1)
                                     or kc == 15))
                    wb_free[kcr] = [mm_last[CH - 1]]
                evs = []
                for mg in range(CH):
                    ev = s.op("dve", lambda E, mg=mg: E.scalar_tensor_tensor(
                        out=SL1[:, 512 * mg:512 * mg + 512], in0=PB[mg][:],
                        scalar=CSC, in1=SL1[:, 512 * mg:512 * mg + 512],
                        op0=OP.mult, op1=OP.add),
                        deps=[mm_last[mg], xld[mg]], sig=True, drain=True)
                    bank_free[mg] = [ev]
                    row0 = g * GT + mg * 128
                    stv = s.dma("sync",
                                xmid_d[row0:row0 + 128,
                                       512 * os:512 * os + 512],
                                SL1[:, 512 * mg:512 * mg + 512], deps=[ev])
                    evs.append(stv)
                sl1_free[0] = evs
                st_evs += evs
            return st_evs

        sl1_free = [[]]
        qw_guard = [[]]

        TV1_cm = nc.sbuf_tensor("TV1", [1, 2 * D], F32)
        TV1 = TV1_cm.__enter__()
        CARKV_cm = nc.sbuf_tensor("CARKV", [1, D], F32)
        CARKV = CARKV_cm.__enter__()
        CARK_cm = nc.sbuf_tensor("CARK", [1, D], F32)
        CARK = CARK_cm.__enter__()
        XT_cm = nc.sbuf_tensor("XT", [128, CH * D], F32)
        XT = XT_cm.__enter__()

        phase_A(0, XT)
        fence("a1")
        xt1_ready = s.barrier(("act",))[0]
        b1 = phase_B(0, XT, xt1_ready,
                     lambda base, fs: TV1[0:1, base + 512 * fs:
                                          base + 512 * fs + 512], [])
        dump("m1", M1[:, :], [128, CH])
        dump("i1", I1[:, :], [128, CH])
        dump("st", ST[:, :], [128, 8])
        dump("sl0a", SL0, [128, 2048])
        dump("yb_a", YB[:], [128, 2048])
        fence("b1")
        dump("xlnT1", XT[:], [128, CH * D])
        dump("phiq1", PHIQ[:], [128, CH * D])
        dump("phik1", PHIK[:], [128, CH * D])
        dump("kv1", KV[:], [128, CH * D])
        dump("tv1", TV1[0:1, :], [1, 2 * D])
        sp_evs = [s.dma("sync", spq_d[:, :], PHIQ[:], deps=b1),
                  s.dma("sync", spk_d[:, :], PHIK[:], deps=b1),
                  s.dma("sync", spkv_d[:, :], KV[:], deps=b1)]

        fence("spill")
        phase_A(1, XT)
        fence("a2")
        xt2_ready = s.barrier(("act",))[0]
        b2 = phase_B(1, XT, xt2_ready,
                     lambda base, fs: (CARKV if base == 0 else CARK)
                     [0:1, 512 * fs:512 * fs + 512], sp_evs)

        fence("b2")
        p1 = s.op("dve", lambda E: E.tensor_tensor(
            out=CARKV[0:1, :], in0=CARKV[0:1, :], in1=TV1[0:1, 0:D],
            op=OP.add), deps=b2 + b1, sig=True, drain=True)
        p2 = s.op("dve", lambda E: E.tensor_tensor(
            out=CARK[0:1, :], in0=CARK[0:1, :], in1=TV1[0:1, D:2 * D],
            op=OP.add), deps=b2 + b1, sig=True, drain=True)
        p3 = s.op("act", lambda E: E.activation(
            out=CARKV[0:1, :], in_=CARKV[0:1, :], func=AF.Copy,
            scale=FLAGE[0:1, 0:1]), deps=[p1, c_fe], sig=True, drain=True)
        p4 = s.op("act", lambda E: E.activation(
            out=CARK[0:1, :], in_=CARK[0:1, :], func=AF.Copy,
            scale=FLAGE[0:1, 0:1]), deps=[p2], sig=True, drain=True)
        di1 = s.dma("sync", spi_d[0:1, 0:D], CARKV[0:1, :], deps=[p3])
        di2 = s.dma("sync", spi_d[0:1, D:2 * D], CARK[0:1, :], deps=[p4])
        ccp = s.cc(lambda E: E.collective_compute(
            "AllReduce", OP.add,
            replica_groups=[[2 * i, 2 * i + 1] for i in range(4)],
            ins=[spi_d[:, :]], outs=[spo_d[:, :]]), deps=[di1, di2])
        do1 = s.dma("sync", CARKV[0:1, :], spo_d[0:1, 0:D], deps=[ccp])
        do2 = s.dma("sync", CARK[0:1, :], spo_d[0:1, D:2 * D], deps=[ccp])
        ci1, ci2 = do1, do2

        fence("cc1")
        dump("carkv2", CARKV[0:1, :], [1, D])
        dump("cark2", CARK[0:1, :], [1, D])
        r2 = phase_C(1, b2, [ci1, ci2], [
            (FLAGOR[:], FLAGO[0:1, 0:1],
             lambda nm, fs: (CARKV if nm == "kv" else CARK)
             [0:1, 512 * fs:512 * fs + 512]),
            (ONES1[:], ONES1[0:1, 0:1],
             lambda nm, fs: TV1[0:1, (0 if nm == "kv" else D) + 512 * fs:
                                (0 if nm == "kv" else D) + 512 * fs + 512]),
        ])
        fence("c2")
        c2_bar = s.barrier(("act", "pe", "dve", "pool"))
        dump("r2", PHIQ[:], [128, CH * D])
        d2 = phase_D(1, XT, r2)

        fence("d2")
        rl_evs = [s.dma("sync", PHIQ[:], spq_d[:, :],
                        deps=c2_bar + s.barrier(("act",))),
                  s.dma("sync", PHIK[:], spk_d[:, :], deps=c2_bar),
                  s.dma("sync", KV[:], spkv_d[:, :], deps=c2_bar)]
        do3 = s.dma("sync", CARKV[0:1, :], spo_d[0:1, 0:D], deps=c2_bar)
        do4 = s.dma("sync", CARK[0:1, :], spo_d[0:1, D:2 * D], deps=c2_bar)
        cj1, cj2 = do3, do4

        fence("rl")
        dump("rlq", PHIQ[:], [128, CH * D])
        dump("rlkv", KV[:], [128, CH * D])
        dump("rlk", PHIK[:], [128, CH * D])
        dump("carkv1", CARKV[0:1, :], [1, D])
        r1 = phase_C(0, rl_evs, [cj1, cj2], [
            (FLAGOR[:], FLAGO[0:1, 0:1],
             lambda nm, fs: (CARKV if nm == "kv" else CARK)
             [0:1, 512 * fs:512 * fs + 512]),
        ])
        fence("c1")
        dump("r1", PHIQ[:], [128, CH * D])
        d1 = phase_D(0, XT, r1)
        car_last = s.barrier(("pe", "act"))
        XT_cm.__exit__(None, None, None)
        CARK_cm.__exit__(None, None, None)
        CARKV_cm.__exit__(None, None, None)
        TV1_cm.__exit__(None, None, None)
        BIASR_cm = nc.sbuf_tensor("BIASR", [1, D], F32)
        BIASR = BIASR_cm.__enter__()
        c_br = s.dma("sync", BIASR[:], bias_d[:, :], deps=car_last)

        fence("d1")
        e_bar = s.barrier(("dve", "act", "pe", "pool"))
        xm_out = nc.dram_tensor("xmid_cp", [TOK, D], F32, kind="ExternalOutput")
        s.dma("sync", xm_out[:, :], xmid_d[:, :], deps=e_bar + d1 + d2)
        spq_out = nc.dram_tensor("spq_cp", [128, CH * D], F32,
                                 kind="ExternalOutput")
        s.dma("sync", spq_out[:, :], spq_d[:, :], deps=e_bar)
        spkv_out = nc.dram_tensor("spkv_cp", [128, CH * D], F32,
                                  kind="ExternalOutput")
        s.dma("sync", spkv_out[:, :], spkv_d[:, :], deps=e_bar)
        spo_out = nc.dram_tensor("spo_cp", [1, 2 * D], F32,
                                 kind="ExternalOutput")
        s.dma("sync", spo_out[:, :], spo_d[:, :], deps=e_bar)
        am_evs = []
        for mc in range(2 * CH):
            row0 = mc * 128
            ld = s.dma("sync", YB[:], xmid_d[row0:row0 + 128, :],
                       deps=yb_free[0] + e_bar + d1 + d2)
            iv = ln_stats(YB[:], M2[:, mc:mc + 1], I2[:, mc:mc + 1], [ld])
            ue = s.op("dve", lambda E, mc=mc: E.tensor_scalar(
                out=SL0, in0=YB[:], scalar1=M2[:, mc:mc + 1],
                scalar2=I2[:, mc:mc + 1], op0=OP.subtract, op1=OP.mult),
                deps=[iv], sig=True)
            yb_free[0] = [ue]
            am = s.op("dve", lambda E, mc=mc: E.tensor_reduce(
                out=AMX[:, mc:mc + 1], in_=SL0, axis=AX.X, op=OP.max,
                apply_absolute_value=True), deps=[ue], sig=True)
            am_evs.append(am)
        amx_ev = s.op("dve", lambda E: E.tensor_reduce(
            out=AMX[:, 0:1], in_=AMX[:, 0:8], axis=AX.X, op=OP.max,
            apply_absolute_value=True), deps=am_evs, sig=True, drain=True)
        cp1 = s.op("pool", lambda E: E.tensor_reduce(
            out=SC[0:1, 0:1], in_=AMX[:, 0:1], axis=AX.C, op=OP.max),
            deps=[amx_ev], sig=True, drain=True)
        cp2 = s.op("pool", lambda E: E.tensor_reduce(
            out=SC[0:1, 1:2], in_=AMW[:, 0:1], axis=AX.C, op=OP.max),
            deps=[amw_ev], sig=True, drain=True)
        cb = s.op("dve", lambda E: E.tensor_tensor(
            out=SC[0:1, 2:3], in0=SC[0:1, 0:1], in1=SC[0:1, 1:2], op=OP.max),
            deps=[cp1, cp2], sig=True, drain=True)
        bc1 = s.op("pool", lambda E: E.tensor_copy(
            CX[0:1, 0:8], SC[0:1, 0:1].broadcast_to((1, 8))), deps=[cp1],
            sig=True, drain=True)
        bc2 = s.op("pool", lambda E: E.tensor_copy(
            CX[0:1, 8:16], SC[0:1, 2:3].broadcast_to((1, 8))), deps=[cb],
            sig=True, drain=True)
        dci = s.dma("sync", cxi_d[:, :], CX[0:1, :], deps=[bc1, bc2])
        ccx = s.cc(lambda E: E.collective_compute(
            "AllReduce", OP.max, replica_groups=[list(range(NC))],
            ins=[cxi_d[:, :]], outs=[cxo_d[:, :]]), deps=[dci])
        dco = s.dma("sync", CX2[0:1, :], cxo_d[:, :], deps=[ccx])
        s1 = s.op("dve", lambda E: E.reciprocal(
            out=SC[0:1, 3:4], in_=CX2[0:1, 0:1]), deps=[dco], sig=True)
        s2 = s.op("dve", lambda E: E.tensor_scalar(
            out=SC[0:1, 4:5], in0=SC[0:1, 3:4], scalar1=K358, scalar2=None,
            op0=OP.mult), deps=[s1], sig=True, drain=True)
        s3 = s.op("dve", lambda E: E.reciprocal(
            out=SC[0:1, 5:6], in_=CX2[0:1, 8:9]), deps=[dco], sig=True)
        s4 = s.op("dve", lambda E: E.tensor_scalar(
            out=SC[0:1, 6:7], in0=SC[0:1, 5:6], scalar1=K358, scalar2=None,
            op0=OP.mult), deps=[s3], sig=True, drain=True)
        s5 = s.op("dve", lambda E: E.tensor_tensor(
            out=SC[0:1, 7:8], in0=CX2[0:1, 0:1], in1=CX2[0:1, 8:9],
            op=OP.mult), deps=[dco], sig=True)
        s6 = s.op("dve", lambda E: E.tensor_scalar(
            out=SC[0:1, 8:9], in0=SC[0:1, 7:8], scalar1=1.0 / (K358 * K358),
            scalar2=None, op0=OP.mult), deps=[s5], sig=True, drain=True)
        s7 = s.op("dve", lambda E: E.reciprocal(
            out=SC[0:1, 9:10], in_=SC[0:1, 8:9]), deps=[s6], sig=True,
            drain=True)
        bsc = s.op("act", lambda E: E.activation(
            out=BIASR[0:1, :], in_=BIASR[0:1, :], func=AF.Copy,
            scale=SC[0:1, 9:10]), deps=[s7, c_br], sig=True, drain=True)
        bevs = []
        for (col, src_ev, scol) in ((1, s2, 4), (2, s4, 6), (3, s6, 8)):
            mm = s.op("pe", lambda E, scol=scol: E.matmul(
                PB[5][:, 0:1], ONES1[:], SC[0:1, scol:scol + 1],
                start=True, stop=True),
                deps=[src_ev, c_o1] + bank_free[5], sig=True)
            cp = s.op("act", lambda E, col=col: E.copy(
                out=B128[:, col:col + 1], in_=PB[5][:, 0:1]), deps=[mm],
                sig=True, drain=True)
            bank_free[5] = [cp]
            bevs.append(cp)

        def emit_quant(src_ap, out_ap, scale_col, sA, sB, sC, sD, deps):
            e1 = s.op("act", lambda E: E.activation(
                out=sA, in_=src_ap, func=AF.Copy,
                scale=B128[:, scale_col:scale_col + 1]), deps=deps, sig=True,
                drain=True)
            e1b = s.op("act", lambda E: E.activation(
                out=sD, in_=sA, func=AF.Copy, scale=PHI4_64),
                deps=[e1], sig=True, drain=True)
            e2 = s.op("dve", lambda E: E.scalar_tensor_tensor(
                out=sC, in0=sA, scalar=PHI4_32, in1=sA,
                op0=OP.mult, op1=OP.subtract), deps=[e1], sig=True)
            e3 = s.op("dve", lambda E: E.scalar_tensor_tensor(
                out=sB, in0=sA, scalar=PHI4_32, in1=sC,
                op0=OP.mult, op1=OP.subtract), deps=[e2], sig=True)
            e5 = s.op("dve", lambda E: E.tensor_tensor(
                out=sC, in0=sD, in1=sA, op=OP.subtract), deps=[e1b, e3],
                sig=True)
            e6 = s.op("dve", lambda E: E.tensor_tensor(
                out=sD, in0=sD, in1=sC, op=OP.subtract), deps=[e5], sig=True)
            e7 = s.op("dve", lambda E: E.tensor_tensor(
                out=sD, in0=sD, in1=sB, op=OP.subtract), deps=[e6], sig=True)
            return s.op("dve", lambda E: E.scalar_tensor_tensor(
                out=out_ap, in0=sD, scalar=B128[:, 0:1], in1=sB,
                op0=OP.mult, op1=OP.add), deps=[e7, ind_ev], sig=True,
                drain=True)

        fence("e")
        dump("sc", SC[0:1, :], [1, 16])
        dump("cx2", CX2[0:1, :], [1, 16])
        dump("b128", B128[:, :], [128, 4])
        dump("m2", M2[:, :], [128, 2 * CH])
        dump("i2", I2[:, :], [128, 2 * CH])
        QX_cm = nc.sbuf_tensor("QX", [128, 16 * TOK], BF16)
        QX = QX_cm.__enter__()
        QW_cm = nc.sbuf_tensor("QW", [128, 4096], BF16)
        QW = QW_cm.__enter__()
        QW2_cm = nc.sbuf_tensor("QW2", [128, 4096], BF16)
        QW2 = QW2_cm.__enter__()
        QWS = [QW, QW2]
        qw_guard2 = [[], []]

        fence("scales")
        qx_evs = []
        for mc in range(2 * CH):
            row0 = mc * 128
            ld = s.dma("sync", YB[:], xmid_d[row0:row0 + 128, :],
                       deps=yb_free[0] + d1 + d2)
            ue = s.op("dve", lambda E, mc=mc: E.tensor_scalar(
                out=SL0, in0=YB[:], scalar1=M2[:, mc:mc + 1],
                scalar2=I2[:, mc:mc + 1], op0=OP.subtract, op1=OP.mult),
                deps=[ld] + qx_evs[-4:], sig=True, drain=True)
            ut_evs = transpose_to(
                lambda j: SL0[:, 128 * j:128 * j + 128],
                lambda jb: SL0[:, 512 * jb:512 * jb + 512],
                [ue])
            qlist = []
            for qq in range(4):
                srcq = SL0[:, 512 * qq:512 * qq + 512]
                sbase = YB if qq % 2 == 0 else SL1
                dst = bass.AP(QX, 4096 * qq + 128 * mc,
                              [[16 * TOK, 128], [TOK, 4], [1, 128]])
                e7 = emit_quant(srcq, dst, 1,
                                sbase[:, 0:512], sbase[:, 512:1024],
                                sbase[:, 1024:1536], sbase[:, 1536:2048],
                                [ut_evs[qq], bevs[0]] + qlist[-2:-1])
                qlist.append(e7)
            qx_evs += qlist[-2:]
            yb_free[0] = qlist[-2:]
        fence("qx")
        qx_done = s.barrier(("dve", "pool", "act"))
        dump("qx", QX[:], [128, 16 * TOK], BF16)

        prevq = [None, None]
        for os in range(4):
            fence(f"f{os}")
            fill_evs = [[] for _ in range(2 * CH)]
            for kcr in range(2):
                wv = (2 * os + kcr) % 2
                qw = QWS[wv]
                wb = WB[kcr]
                src = bass.AP(
                    wffn_d, os * 512 + kcr * 8 * 128 * D,
                    [[D, 128], [128 * D, 8], [1, 512]])
                dd = s.dma("sync",
                           wb[:].rearrange("p (k c) -> p k c", k=8), src,
                           deps=wb_free[kcr] + (qx_done if os == 0 else []))
                qdone = []
                for qq in range(8):
                    srcq = wb[:, 512 * qq:512 * qq + 512]
                    sbase = SL0 if qq % 2 == 0 else SL1
                    xdeps = [dd, bevs[1]] + qdone[-2:-1] + qx_done
                    if qq < 2:
                        xdeps += qw_guard2[wv]
                        if prevq[0] is not None:
                            xdeps += [prevq[0][6 + qq]]
                    w7 = emit_quant(
                        srcq, qw[:, 512 * qq:512 * qq + 512], 2,
                        sbase[:, 0:512], sbase[:, 512:1024],
                        sbase[:, 1024:1536], sbase[:, 1536:2048],
                        xdeps)
                    qdone.append(w7)
                prevq[0] = qdone
                if os == 0 and kcr == 0:
                    dump("qw00", qw[:], [128, 4096], BF16)
                mm_last = [None] * (2 * CH)
                for kc8 in range(8):
                    kc = kcr * 8 + kc8
                    for mc in range(2 * CH):
                        st = (kc == 0)
                        deps = [qdone[kc8]]
                        if st:
                            deps += bank_free[mc]
                        mm_last[mc] = s.op(
                            "pe", lambda E, mc=mc, kc=kc, kc8=kc8, qw=qw:
                            E.matmul(
                                PB[mc][:],
                                QX[:, TOK * kc + 128 * mc:
                                   TOK * kc + 128 * mc + 128],
                                qw[:, 512 * kc8:512 * kc8 + 512],
                                start=(kc == 0), stop=False),
                            deps=deps,
                            sig=((kc8 == 7 and mc == 2 * CH - 1)
                                 or kc == 15))
                wb_free[kcr] = [mm_last[2 * CH - 1]]
                for mc in range(2 * CH):
                    fill_evs[mc].append(mm_last[mc])
                qw_guard2[wv] = [mm_last[2 * CH - 1]]
                if os == 0 and kcr == 0:
                    dump("qw00b", qw[:], [128, 4096], BF16)
                    dump("qx7", QX[:, 7168:7168 + 1024], [128, 1024], BF16)
            for mc in range(2 * CH):
                bm = s.op("pe", lambda E, mc=mc, os=os: E.matmul(
                    PB[mc][:], ONES1[:],
                    BIASR[0:1, 512 * os:512 * os + 512],
                    start=False, stop=True),
                    deps=[bsc] + fill_evs[mc], sig=True)
                fill_evs[mc].append(bm)
            if os == 0 and "psum00" in DBG:
                pd = dbg_decl("psum00", [128, 512])
                cpev = s.op("act", lambda E: E.copy(
                    out=SL1[:, 0:512], in_=PB[0][:]),
                    deps=fill_evs[0][-1:], sig=True)
                s.dma("sync", pd[:, :], SL1[:, 0:512], deps=[cpev])
                s.gate = s.barrier(("pe", "dve", "act", "pool")) +                     [("dsync", s.cnt["dsync"])]
            for mc in range(2 * CH):
                row0 = mc * 128
                xld = s.dma("sync", YB[:, 512 * os:512 * os + 512],
                            xmid_d[row0:row0 + 128, 512 * os:512 * os + 512],
                            deps=yb_free[0] + d1 + d2)
                ev = s.op("dve", lambda E, mc=mc, os=os:
                          E.scalar_tensor_tensor(
                              out=YB[:, 512 * os:512 * os + 512],
                              in0=PB[mc][:], scalar=B128[:, 3:4],
                              in1=YB[:, 512 * os:512 * os + 512],
                              op0=OP.mult, op1=OP.add),
                          deps=[xld, bevs[2]] + fill_evs[mc][-1:], sig=True,
                          drain=True)
                bank_free[mc] = [ev]
                if os == 0 and mc == 0:
                    dump("ffn00", YB[:, 0:512], [128, 512])
                sto = s.dma("sync",
                            y_d[row0:row0 + 128, 512 * os:512 * os + 512],
                            YB[:, 512 * os:512 * os + 512], deps=[ev])
                yb_free[0] = [sto]

        QW2_cm.__exit__(None, None, None)
        QW_cm.__exit__(None, None, None)
        QX_cm.__exit__(None, None, None)
        BIASR_cm.__exit__(None, None, None)

        last_out = ("dpool", s.cnt["dpool"])
        s.op("pool", lambda E: E.engine_nop(), deps=[last_out])

        with nc.Block() as block:
            s.emit(block, semh)
    return nc


_NC_CACHE = {}


def kernel(L0_attn_x, L0_attn_W_qkv, L0_attn_W_out, L0_ffn_weight,
           L0_ffn_bias):
    x = np.ascontiguousarray(
        np.asarray(L0_attn_x, np.float32)).reshape(NC * TOK, D)
    wqkvT = np.ascontiguousarray(np.asarray(L0_attn_W_qkv, np.float32).T)
    woutT = np.ascontiguousarray(np.asarray(L0_attn_W_out, np.float32).T)
    wffnT = np.ascontiguousarray(np.asarray(L0_ffn_weight, np.float32).T)
    biasr = np.ascontiguousarray(
        np.asarray(L0_ffn_bias, np.float32)).reshape(1, D)
    tri = np.triu(np.ones((128, 128), np.float32))
    ident = np.eye(128, dtype=np.float32)
    ones1 = np.ones((1, 128), np.float32)
    onesc = np.ones((128, 1), np.float32)

    if "nc" not in _NC_CACHE:
        _NC_CACHE["nc"] = build_program()
    nc = _NC_CACHE["nc"]
    in_maps = []
    for c in range(NC):
        in_maps.append({
            "x": x[c * TOK:(c + 1) * TOK],
            "wqkvT": wqkvT, "woutT": woutT, "wffnT": wffnT, "biasr": biasr,
            "tri": tri, "ident": ident, "ones1": ones1, "onesc": onesc,
            "flago": np.full((1, 1), float(c % 2), np.float32),
            "flagor": np.full((1, 128), float(c % 2), np.float32),
            "flage": np.full((1, 1), float(1 - c % 2), np.float32),
        })
    res = run_bass_kernel_spmd(nc, in_maps, core_ids=list(range(NC)))
    y = np.concatenate([np.asarray(r["y"]) for r in res.results], axis=0)
    return y.reshape(4, 2048, 2048)


# version 07 (final)
# speedup vs baseline: 1.2274x; 1.0241x over previous
import numpy as np
import concourse.bass as bass
import concourse.mybir as mybir
from concourse.bass_utils import run_bass_kernel_spmd

F32 = mybir.dt.float32
BF16 = mybir.dt.bfloat16
AF = mybir.ActivationFunctionType
OP = mybir.AluOpType
AX = mybir.AxisListType

D = 2048
TOK = 1024
G = 2
GT = TOK // G
CH = GT // 128
NC = 8

PHI4_32 = float(2 ** 20 + 1)
PHI4_64 = float(2 ** 49 + 1)
K358 = 448.0 * 0.8
EPS_LN = 1e-5
EPS_ATTN = 1e-9
CSC = 128.0 ** -0.5


class Sched:
    ENG2BLOCK = {"pe": "tensor", "dve": "vector", "act": "scalar",
                 "pool": "gpsimd", "sync": "sync"}

    def __init__(self):
        self.q = {e: [] for e in self.ENG2BLOCK}
        self.cnt = {"pe": 0, "dve": 0, "act": 0, "pool": 0,
                    "dsync": 0, "dpool": 0, "cc": 0}
        self.gate = []

    def op(self, eng, fn, deps=(), sig=False, drain=False):
        deps = [d for d in deps if d is not None] + self.gate
        ev = None
        if sig:
            self.cnt[eng] += 1
            ev = (eng, self.cnt[eng])
        self.q[eng].append([deps, fn, ("c", ev) if ev else None,
                            drain and sig])
        return ev

    def dma(self, eng, out, in_, deps=()):
        sem = "dsync" if eng == "sync" else "dpool"
        deps = [d for d in deps if d is not None] + self.gate
        self.cnt[sem] += 16
        ev = (sem, self.cnt[sem])
        self.q[eng].append(
            [deps, (lambda E, o=out, i=in_: E.dma_start(out=o, in_=i)),
             ("d", ev), False])
        return ev

    def cc(self, fn, deps=()):
        deps = [d for d in deps if d is not None]
        self.cnt["cc"] += 1
        ev = ("cc", self.cnt["cc"])
        self.q["pool"].append([deps, fn, ("c", ev), False])
        return ev

    def barrier(self, engines=("pe", "dve", "act", "pool")):
        evs = []
        for e in engines:
            if not self.q[e]:
                continue
            last = self.q[e][-1]
            if last[2] is None:
                self.cnt[e] += 1
                ev = (e, self.cnt[e])
                last[2] = ("c", ev)
                last[3] = True
            else:
                ev = last[2][1]
            evs.append(ev)
        return evs

    def emit(self, block, semh):
        for eng, bname in self.ENG2BLOCK.items():
            items = self.q[eng]
            own = eng

            def body(E, items=items, own=own):
                waited = {}
                for deps, fn, sig, drn in items:
                    for (sname, v) in deps:
                        if waited.get(sname, 0) < v:
                            E.wait_ge(semh[sname], v)
                            waited[sname] = v
                    ins = fn(E)
                    if sig is not None:
                        kind, (sname, v) = sig
                        if (drn or own == "pe") and kind == "c":
                            ins = E.drain()
                        ins.then_inc(semh[sname], 16 if kind == "d" else 1)

            getattr(block, bname)(body)


DBG = set()
FENCES = set()


def build_program():
    nc = bass.Bass("TRN2", target_bir_lowering=False,
                   dynamic_dma_scratch_size=4096)
    dbg_outs = {}

    def dbg_decl(name, shape, dt=F32):
        dbg_outs[name] = nc.dram_tensor("dbg_" + name, shape, dt,
                                        kind="ExternalOutput")
        return dbg_outs[name]

    x_d = nc.dram_tensor("x", [TOK, D], F32, kind="ExternalInput")
    wqkv_d = nc.dram_tensor("wqkvT", [D, 3 * D], F32, kind="ExternalInput")
    wout_d = nc.dram_tensor("woutT", [D, D], F32, kind="ExternalInput")
    wffn_d = nc.dram_tensor("wffnT", [D, D], F32, kind="ExternalInput")
    bias_d = nc.dram_tensor("biasr", [1, D], F32, kind="ExternalInput")
    tri_d = nc.dram_tensor("tri", [128, 128], F32, kind="ExternalInput")
    id_d = nc.dram_tensor("ident", [128, 128], F32, kind="ExternalInput")
    ones1_d = nc.dram_tensor("ones1", [1, 128], F32, kind="ExternalInput")
    onesc_d = nc.dram_tensor("onesc", [128, 1], F32, kind="ExternalInput")
    flago_d = nc.dram_tensor("flago", [1, 1], F32, kind="ExternalInput")
    flagor_d = nc.dram_tensor("flagor", [1, 128], F32, kind="ExternalInput")
    flage_d = nc.dram_tensor("flage", [1, 1], F32, kind="ExternalInput")
    y_d = nc.dram_tensor("y", [TOK, D], F32, kind="ExternalOutput")
    xmid_d = nc.dram_tensor("xmid_i", [TOK, D], F32)
    spq_d = nc.dram_tensor("spq_i", [128, CH * D], F32)
    spk_d = nc.dram_tensor("spk_i", [128, CH * D], F32)
    spkv_d = nc.dram_tensor("spkv_i", [128, CH * D], F32)
    spi_d = nc.dram_tensor("spi_i", [1, 2 * D], F32)
    spo_d = nc.dram_tensor("spo_i", [1, 2 * D], F32)
    cxi_d = nc.dram_tensor("cxi_i", [1, 16], F32)
    cxo_d = nc.dram_tensor("cxo_i", [1, 16], F32, addr_space="Shared")

    from contextlib import ExitStack
    s = Sched()
    with ExitStack() as es:
        e = es.enter_context
        PHIQ = e(nc.sbuf_tensor("PHIQ", [128, CH * D], F32))
        KV = e(nc.sbuf_tensor("KV", [128, CH * D], F32))
        PHIK = e(nc.sbuf_tensor("PHIK", [128, CH * D], F32))
        WB = [e(nc.sbuf_tensor(f"WB{i}", [128, 4096], F32)) for i in range(2)]
        QSCR = e(nc.sbuf_tensor("QSCR", [128, 4096], F32))
        YB = e(nc.sbuf_tensor("YB", [128, D], F32))
        TRI = e(nc.sbuf_tensor("TRI", [128, 128], F32))
        IDN = e(nc.sbuf_tensor("IDN", [128, 128], F32))
        ONES1 = e(nc.sbuf_tensor("ONES1", [1, 128], F32))
        ONESC = e(nc.sbuf_tensor("ONESC", [128, 1], F32))
        FLAGO = e(nc.sbuf_tensor("FLAGO", [1, 1], F32))
        FLAGOR = e(nc.sbuf_tensor("FLAGOR", [1, 128], F32))
        FLAGE = e(nc.sbuf_tensor("FLAGE", [1, 1], F32))
        CX = e(nc.sbuf_tensor("CX", [1, 16], F32))
        CX2 = e(nc.sbuf_tensor("CX2", [1, 16], F32))
        SC = e(nc.sbuf_tensor("SC", [1, 16], F32))
        ST = e(nc.sbuf_tensor("ST", [128, 8], F32))
        M1 = e(nc.sbuf_tensor("M1", [128, CH], F32))
        I1 = e(nc.sbuf_tensor("I1", [128, CH], F32))
        M2 = e(nc.sbuf_tensor("M2", [128, 2 * CH], F32))
        I2 = e(nc.sbuf_tensor("I2", [128, 2 * CH], F32))
        AMX = e(nc.sbuf_tensor("AMX", [128, 8], F32))
        AMW = e(nc.sbuf_tensor("AMW", [128, 16], F32))
        B128 = e(nc.sbuf_tensor("B128", [128, 4], F32))
        PB = [e(nc.psum_tensor(f"PB{i}", [128, 512], F32)) for i in range(8)]
        semh = {n: e(nc.semaphore(n)) for n in
                ("pe", "dve", "act", "pool", "dsync", "dpool", "cc")}

        SL0 = QSCR[:, 0:2048]
        SL1 = QSCR[:, 2048:4096]

        bank_free = [[] for _ in range(8)]
        wb_free = [[], []]
        yb_free = [[]]

        def fence(name):
            if name != "rl" and "ALL" not in FENCES and name not in FENCES:
                return
            evs = s.barrier(("pe", "dve", "act", "pool"))
            evs += [("dsync", s.cnt["dsync"]), ("dpool", s.cnt["dpool"])]
            if s.cnt["cc"]:
                evs.append(("cc", s.cnt["cc"]))
            s.gate = evs

        def dump(name, src_ap, shape, dt=F32):
            if name not in DBG:
                return
            dd = dbg_decl(name, shape, dt)
            ev = s.dma("sync", dd[tuple(slice(0, x) for x in shape)], src_ap,
                       deps=s.barrier(("pe", "dve", "act", "pool")))
            s.gate = [ev]

        c_tri = s.dma("sync", TRI[:], tri_d[:, :])
        c_id = s.dma("sync", IDN[:], id_d[:, :])
        c_o1 = s.dma("sync", ONES1[:], ones1_d[:, :])
        c_oc = s.dma("sync", ONESC[:], onesc_d[:, :])
        c_fo = s.dma("sync", FLAGO[:], flago_d[:, :])
        c_for = s.dma("sync", FLAGOR[:], flagor_d[:, :])
        c_fe = s.dma("sync", FLAGE[:], flage_d[:, :])

        iv0 = s.op("dve", lambda E: E.memset(B128[:, 0:1], 1.0), sig=True)
        iv1 = s.op("dve", lambda E: E.tensor_scalar(
            out=B128[:, 0:1], in0=B128[:, 0:1], scalar1=float(2.0 ** -30),
            scalar2=None, op0=OP.add), deps=[iv0], sig=True)
        ind_ev = s.op("dve", lambda E: E.tensor_scalar(
            out=B128[:, 0:1], in0=B128[:, 0:1], scalar1=1.0,
            scalar2=float(2.0 ** 30), op0=OP.subtract, op1=OP.mult),
            deps=[iv1], sig=True)

        amw_parts = []
        for i in range(8):
            src = bass.AP(wffn_d, i * 256, [[D, 128], [128 * D, 16], [1, 256]])
            dst = WB[i % 2][:].rearrange("p (k c) -> p k c", k=16)
            ld = s.dma("sync", dst, src, deps=wb_free[i % 2])
            rd = s.op("dve", lambda E, i=i, b=i % 2: E.tensor_reduce(
                out=AMW[:, i:i + 1], in_=WB[b][:], axis=AX.X, op=OP.max,
                apply_absolute_value=True), deps=[ld], sig=True)
            wb_free[i % 2] = [rd]
            amw_parts.append(rd)
        amw_ev = s.op("dve", lambda E: E.tensor_reduce(
            out=AMW[:, 0:1], in_=AMW[:, 0:8], axis=AX.X, op=OP.max,
            apply_absolute_value=True), deps=amw_parts, sig=True, drain=True)

        def ln_stats(chunk_src, mean_ap, inv_ap, deps):
            r1 = s.op("dve", lambda E: E.tensor_reduce(
                out=ST[:, 0:1], in_=chunk_src, axis=AX.X, op=OP.add),
                deps=deps, sig=True)
            sq = s.op("act", lambda E: E.activation(
                out=SL1, in_=chunk_src, func=AF.Square), deps=deps, sig=True,
                drain=True)
            r2 = s.op("dve", lambda E: E.tensor_reduce(
                out=ST[:, 1:2], in_=SL1, axis=AX.X, op=OP.add),
                deps=[sq], sig=True)
            g1 = s.op("dve", lambda E: E.tensor_scalar(
                out=mean_ap, in0=ST[:, 0:1], scalar1=1.0 / D, scalar2=None,
                op0=OP.mult), deps=[r1], sig=True)
            g2 = s.op("dve", lambda E: E.tensor_scalar(
                out=ST[:, 2:3], in0=ST[:, 1:2], scalar1=1.0 / D, scalar2=None,
                op0=OP.mult), deps=[r2], sig=True)
            g3 = s.op("dve", lambda E: E.tensor_tensor(
                out=ST[:, 3:4], in0=mean_ap, in1=mean_ap, op=OP.mult),
                deps=[g1], sig=True)
            g4 = s.op("dve", lambda E: E.tensor_tensor(
                out=ST[:, 4:5], in0=ST[:, 2:3], in1=ST[:, 3:4],
                op=OP.subtract), deps=[g2, g3], sig=True)
            ve = s.op("dve", lambda E: E.tensor_scalar(
                out=ST[:, 5:6], in0=ST[:, 4:5], scalar1=EPS_LN, scalar2=None,
                op0=OP.add), deps=[g4], sig=True)
            sd = s.op("act", lambda E: E.activation(
                out=ST[:, 6:7], in_=ST[:, 5:6], func=AF.Sqrt), deps=[ve],
                sig=True, drain=True)
            return s.op("dve", lambda E: E.reciprocal(
                out=inv_ap, in_=ST[:, 6:7]), deps=[sd], sig=True)

        def transpose_to(src_tile_fn, dst_fn, deps):
            evs = []
            for jb in range(4):
                bidx = 6 + (jb % 2)
                bank = PB[bidx]
                last = None
                for t in range(4):
                    j = 4 * jb + t
                    dd = (deps + bank_free[bidx] + [c_id]) if t == 0 else []
                    last = s.op("pe", lambda E, j=j, t=t, b=bank: E.transpose(
                        b[:, 128 * t:128 * t + 128], src_tile_fn(j), IDN[:]),
                        deps=dd, sig=(t == 3))
                ev = s.op("act", lambda E, jb=jb, b=bank: E.copy(
                    out=dst_fn(jb), in_=b[:, 0:512]), deps=[last], sig=True)
                bank_free[bidx] = [ev]
                evs.append(ev)
            return evs

        def phase_A(g, XT):
            tp_evs = []
            for mg in range(CH):
                row0 = g * GT + mg * 128
                ld = s.dma("sync", YB[:], x_d[row0:row0 + 128, :],
                           deps=yb_free[0])
                iv = ln_stats(YB[:], M1[:, mg:mg + 1], I1[:, mg:mg + 1], [ld])
                ap = s.op("dve", lambda E, mg=mg: E.tensor_scalar(
                    out=SL0, in0=YB[:], scalar1=M1[:, mg:mg + 1],
                    scalar2=I1[:, mg:mg + 1], op0=OP.subtract, op1=OP.mult),
                    deps=[iv] + tp_evs[-4:], sig=True, drain=True)
                yb_free[0] = [ap]
                tp_evs = transpose_to(
                    lambda j: SL0[:, 128 * j:128 * j + 128],
                    lambda jb, mg=mg: bass.AP(
                        XT, 2048 * jb + 128 * mg,
                        [[CH * D, 128], [512, 4], [1, 128]]),
                    [ap])
            return tp_evs

        def phase_B(g, XT, xt_ready, totg_fn, extra):
            for fs in range(4):
                phk_evs = [None] * CH
                kv_evs = [None] * CH
                for tgt in range(3):
                    col0 = tgt * D + fs * 512
                    mm_last = [None] * CH
                    for kcr in range(2):
                        wb = WB[kcr]
                        src = bass.AP(
                            wqkv_d, col0 + kcr * 8 * 128 * 3 * D,
                            [[3 * D, 128], [128 * 3 * D, 8], [1, 512]])
                        dd = s.dma("sync",
                                   wb[:].rearrange("p (k c) -> p k c", k=8),
                                   src, deps=wb_free[kcr])
                        for kc8 in range(8):
                            kc = kcr * 8 + kc8
                            for mg in range(CH):
                                st = (kc == 0)
                                deps = [dd]
                                if st:
                                    deps += bank_free[mg] + [xt_ready]
                                mm_last[mg] = s.op(
                                    "pe",
                                    lambda E, mg=mg, kc=kc, kc8=kc8,
                                    wb=wb, XT=XT: E.matmul(
                                        PB[mg][:],
                                        XT[:, 512 * kc + 128 * mg:
                                           512 * kc + 128 * mg + 128],
                                        wb[:, 512 * kc8:512 * kc8 + 512],
                                        start=(kc == 0), stop=(kc == 15)),
                                    deps=deps,
                                    sig=((kc8 == 7 and mg == CH - 1)
                                         or kc == 15))
                        wb_free[kcr] = [mm_last[CH - 1]]
                    for mg in range(CH):
                        bank = PB[mg]
                        dsl = slice(D * mg + 512 * fs, D * mg + 512 * fs + 512)
                        if tgt < 2:
                            out = PHIQ if tgt == 0 else PHIK
                            t1 = s.op("dve", lambda E, bank=bank: E.tensor_scalar(
                                out=SL0[:, 0:512], in0=bank[:], scalar1=0.0,
                                scalar2=None, op0=OP.min),
                                deps=[mm_last[mg]], sig=True)
                            ex = s.op("act", lambda E: E.activation(
                                out=SL0[:, 512:1024], in_=SL0[:, 0:512],
                                func=AF.Exp), deps=[t1], sig=True, drain=True)
                            rl = s.op("act", lambda E, bank=bank: E.activation(
                                out=SL0[:, 1024:1536], in_=bank[:],
                                func=AF.Relu), deps=[mm_last[mg]], sig=True,
                                drain=True)
                            ph = s.op("dve",
                                      lambda E, out=out, dsl=dsl: E.tensor_tensor(
                                          out=out[:, dsl], in0=SL0[:, 512:1024],
                                          in1=SL0[:, 1024:1536], op=OP.add),
                                      deps=[ex, rl] + extra, sig=True,
                                      drain=True)
                            bank_free[mg] = [t1, rl]
                            if tgt == 1:
                                phk_evs[mg] = ph
                        else:
                            kve = s.op("dve",
                                       lambda E, dsl=dsl, bank=bank:
                                       E.tensor_tensor(
                                           out=KV[:, dsl], in0=PHIK[:, dsl],
                                           in1=bank[:], op=OP.mult),
                                       deps=[mm_last[mg], phk_evs[mg]] + extra,
                                       sig=True, drain=True)
                            bank_free[mg] = [kve]
                            kv_evs[mg] = kve
                for (tens, base, sevs) in ((KV, 0, kv_evs), (PHIK, D, phk_evs)):
                    mms = []
                    for mg in range(CH):
                        mms.append(s.op(
                            "pe", lambda E, tens=tens, mg=mg, fs=fs: E.matmul(
                                PB[4][0:1, 0:512], ONESC[:],
                                tens[:, D * mg + 512 * fs:
                                     D * mg + 512 * fs + 512],
                                start=(mg == 0), stop=(mg == CH - 1)),
                            deps=(sevs + bank_free[4] + [c_oc])
                            if mg == 0 else [], sig=(mg == CH - 1)))
                    tev = s.op("act", lambda E, base=base, fs=fs,
                               totg_fn=totg_fn: E.copy(
                                   out=totg_fn(base, fs), in_=PB[4][0:1, 0:512]),
                               deps=[mms[-1]], sig=True, drain=True)
                    bank_free[4] = [tev]
            return s.barrier(("dve", "act", "pe"))

        def phase_C(g, src_deps, car_init, car0):
            r_evs = []
            car_ev = {}
            for fs in range(4):
                car_ev[("kv", fs)] = list(car_init)
                car_ev[("k", fs)] = list(car_init)
            for mg in range(CH):
                for fs in range(4):
                    par = fs % 2
                    SLP = SL0 if par == 0 else YB
                    sl = slice(D * mg + 512 * fs, D * mg + 512 * fs + 512)
                    evd = {}
                    for nm, tens, car, bi in (
                            ("kv", KV, CARKV, 4 if par == 0 else 0),
                            ("k", PHIK, CARK, 5 if par == 0 else 1)):
                        bc = bi + 2
                        s.op("pe", lambda E, tens=tens, sl=sl, bi=bi:
                             E.matmul(PB[bi][:], TRI[:], tens[:, sl],
                                      start=True, stop=False),
                             deps=bank_free[bi] + src_deps + [c_tri])
                        if mg == 0:
                            terms = [(row, scl, rfn(nm, fs))
                                     for (row, scl, rfn) in car0]
                        else:
                            terms = [(ONES1[:], ONES1[0:1, 0:1],
                                      car[0:1, 512 * fs:512 * fs + 512])]
                        t2 = t3 = None
                        nt = len(terms)
                        for ti, (row, scl, rhs) in enumerate(terms):
                            t2 = s.op("pe", lambda E, row=row, rhs=rhs, bi=bi,
                                      last=(ti == nt - 1):
                                      E.matmul(PB[bi][:], row, rhs,
                                               start=False, stop=last),
                                      deps=car_ev[(nm, fs)] + [c_o1, c_for],
                                      sig=(ti == nt - 1))
                        s.op("pe", lambda E, tens=tens, sl=sl, bc=bc:
                             E.matmul(PB[bc][0:1, 0:512], ONESC[:],
                                      tens[:, sl], start=True, stop=False),
                             deps=bank_free[bc] + [c_oc])
                        for ti, (row, scl, rhs) in enumerate(terms):
                            t3 = s.op("pe", lambda E, scl=scl, rhs=rhs, bc=bc,
                                      last=(ti == nt - 1):
                                      E.matmul(PB[bc][0:1, 0:512], scl, rhs,
                                               start=False, stop=last),
                                      deps=car_ev[(nm, fs)],
                                      sig=(ti == nt - 1))
                        cx = s.op("act", lambda E, car=car, fs=fs, bc=bc:
                                  E.copy(out=car[0:1, 512 * fs:512 * fs + 512],
                                         in_=PB[bc][0:1, 0:512]),
                                  deps=[t3, t2], sig=True, drain=True)
                        bank_free[bc] = [cx]
                        car_ev[(nm, fs)] = [cx]
                        evd[nm] = t2
                        evd[nm + "x"] = cx
                    bkv, bk = (4, 5) if par == 0 else (0, 1)
                    num = s.op("dve", lambda E, sl=sl, bkv=bkv, SLP=SLP: E.tensor_tensor(
                        out=SLP[:, 0:512], in0=PHIQ[:, sl], in1=PB[bkv][:],
                        op=OP.mult), deps=[evd["kv"]], sig=True)
                    den = s.op("dve", lambda E, sl=sl, bk=bk, SLP=SLP: E.tensor_tensor(
                        out=SLP[:, 512:1024], in0=PHIQ[:, sl], in1=PB[bk][:],
                        op=OP.mult), deps=[evd["k"]], sig=True)
                    bank_free[bkv] = [num, evd["kvx"]]
                    bank_free[bk] = [den, evd["kx"]]
                    de = s.op("act", lambda E, SLP=SLP: E.activation(
                        out=SLP[:, 512:1024], in_=SLP[:, 512:1024],
                        func=AF.Copy, bias=EPS_ATTN, scale=1.0),
                        deps=[den], sig=True, drain=True)
                    rc = s.op("dve", lambda E, SLP=SLP: E.reciprocal(
                        out=SLP[:, 1024:1536], in_=SLP[:, 512:1024]),
                        deps=[de], sig=True)
                    rv = s.op("dve", lambda E, sl=sl, SLP=SLP: E.tensor_tensor(
                        out=PHIQ[:, sl], in0=SLP[:, 0:512],
                        in1=SLP[:, 1024:1536], op=OP.mult), deps=[rc],
                        sig=True, drain=True)
                    r_evs.append(rv)
            return r_evs

        def phase_D(g, RT, r_deps):
            tp = []
            for mg in range(CH):
                tp += transpose_to(
                    lambda j, mg=mg: PHIQ[:, D * mg + 128 * j:
                                          D * mg + 128 * j + 128],
                    lambda jb, mg=mg: bass.AP(
                        RT, 2048 * jb + 128 * mg,
                        [[CH * D, 128], [512, 4], [1, 128]]),
                    r_deps)
            rt_ready = s.barrier(("act",))
            st_evs = []
            for os in range(4):
                xld = [None] * CH
                for mg in range(CH):
                    row0 = g * GT + mg * 128
                    xld[mg] = s.dma(
                        "sync", SL1[:, 512 * mg:512 * mg + 512],
                        x_d[row0:row0 + 128, 512 * os:512 * os + 512],
                        deps=sl1_free[0])
                mm_last = [None] * CH
                for kcr in range(2):
                    wb = WB[kcr]
                    src = bass.AP(
                        wout_d, os * 512 + kcr * 8 * 128 * D,
                        [[D, 128], [128 * D, 8], [1, 512]])
                    dd = s.dma("sync",
                               wb[:].rearrange("p (k c) -> p k c", k=8),
                               src, deps=wb_free[kcr])
                    for kc8 in range(8):
                        kc = kcr * 8 + kc8
                        for mg in range(CH):
                            st = (kc == 0)
                            deps = [dd]
                            if st:
                                deps += bank_free[mg] + rt_ready
                            mm_last[mg] = s.op(
                                "pe", lambda E, mg=mg, kc=kc, kc8=kc8,
                                wb=wb, RT=RT: E.matmul(
                                    PB[mg][:],
                                    RT[:, 512 * kc + 128 * mg:
                                       512 * kc + 128 * mg + 128],
                                    wb[:, 512 * kc8:512 * kc8 + 512],
                                    start=(kc == 0), stop=(kc == 15)),
                                deps=deps,
                                sig=((kc8 == 7 and mg == CH - 1)
                                     or kc == 15))
                    wb_free[kcr] = [mm_last[CH - 1]]
                evs = []
                for mg in range(CH):
                    ev = s.op("dve", lambda E, mg=mg: E.scalar_tensor_tensor(
                        out=SL1[:, 512 * mg:512 * mg + 512], in0=PB[mg][:],
                        scalar=CSC, in1=SL1[:, 512 * mg:512 * mg + 512],
                        op0=OP.mult, op1=OP.add),
                        deps=[mm_last[mg], xld[mg]], sig=True, drain=True)
                    bank_free[mg] = [ev]
                    row0 = g * GT + mg * 128
                    stv = s.dma("sync",
                                xmid_d[row0:row0 + 128,
                                       512 * os:512 * os + 512],
                                SL1[:, 512 * mg:512 * mg + 512], deps=[ev])
                    evs.append(stv)
                sl1_free[0] = evs
                st_evs += evs
            return st_evs

        sl1_free = [[]]
        qw_guard = [[]]

        TV1_cm = nc.sbuf_tensor("TV1", [1, 2 * D], F32)
        TV1 = TV1_cm.__enter__()
        CARKV_cm = nc.sbuf_tensor("CARKV", [1, D], F32)
        CARKV = CARKV_cm.__enter__()
        CARK_cm = nc.sbuf_tensor("CARK", [1, D], F32)
        CARK = CARK_cm.__enter__()
        XT_cm = nc.sbuf_tensor("XT", [128, CH * D], F32)
        XT = XT_cm.__enter__()

        phase_A(0, XT)
        fence("a1")
        xt1_ready = s.barrier(("act",))[0]
        b1 = phase_B(0, XT, xt1_ready,
                     lambda base, fs: TV1[0:1, base + 512 * fs:
                                          base + 512 * fs + 512], [])
        dump("m1", M1[:, :], [128, CH])
        dump("i1", I1[:, :], [128, CH])
        dump("st", ST[:, :], [128, 8])
        dump("sl0a", SL0, [128, 2048])
        dump("yb_a", YB[:], [128, 2048])
        fence("b1")
        dump("xlnT1", XT[:], [128, CH * D])
        dump("phiq1", PHIQ[:], [128, CH * D])
        dump("phik1", PHIK[:], [128, CH * D])
        dump("kv1", KV[:], [128, CH * D])
        dump("tv1", TV1[0:1, :], [1, 2 * D])
        sp_evs = [s.dma("sync", spq_d[:, :], PHIQ[:], deps=b1),
                  s.dma("sync", spk_d[:, :], PHIK[:], deps=b1),
                  s.dma("sync", spkv_d[:, :], KV[:], deps=b1)]

        fence("spill")
        phase_A(1, XT)
        fence("a2")
        xt2_ready = s.barrier(("act",))[0]
        b2 = phase_B(1, XT, xt2_ready,
                     lambda base, fs: (CARKV if base == 0 else CARK)
                     [0:1, 512 * fs:512 * fs + 512], sp_evs)

        fence("b2")
        p1 = s.op("dve", lambda E: E.tensor_tensor(
            out=CARKV[0:1, :], in0=CARKV[0:1, :], in1=TV1[0:1, 0:D],
            op=OP.add), deps=b2 + b1, sig=True, drain=True)
        p2 = s.op("dve", lambda E: E.tensor_tensor(
            out=CARK[0:1, :], in0=CARK[0:1, :], in1=TV1[0:1, D:2 * D],
            op=OP.add), deps=b2 + b1, sig=True, drain=True)
        p3 = s.op("act", lambda E: E.activation(
            out=CARKV[0:1, :], in_=CARKV[0:1, :], func=AF.Copy,
            scale=FLAGE[0:1, 0:1]), deps=[p1, c_fe], sig=True, drain=True)
        p4 = s.op("act", lambda E: E.activation(
            out=CARK[0:1, :], in_=CARK[0:1, :], func=AF.Copy,
            scale=FLAGE[0:1, 0:1]), deps=[p2], sig=True, drain=True)
        di1 = s.dma("sync", spi_d[0:1, 0:D], CARKV[0:1, :], deps=[p3])
        di2 = s.dma("sync", spi_d[0:1, D:2 * D], CARK[0:1, :], deps=[p4])
        ccp = s.cc(lambda E: E.collective_compute(
            "AllReduce", OP.add,
            replica_groups=[[2 * i, 2 * i + 1] for i in range(4)],
            ins=[spi_d[:, :]], outs=[spo_d[:, :]]), deps=[di1, di2])
        do1 = s.dma("sync", CARKV[0:1, :], spo_d[0:1, 0:D], deps=[ccp])
        do2 = s.dma("sync", CARK[0:1, :], spo_d[0:1, D:2 * D], deps=[ccp])
        ci1, ci2 = do1, do2

        fence("cc1")
        dump("carkv2", CARKV[0:1, :], [1, D])
        dump("cark2", CARK[0:1, :], [1, D])
        r2 = phase_C(1, b2, [ci1, ci2], [
            (FLAGOR[:], FLAGO[0:1, 0:1],
             lambda nm, fs: (CARKV if nm == "kv" else CARK)
             [0:1, 512 * fs:512 * fs + 512]),
            (ONES1[:], ONES1[0:1, 0:1],
             lambda nm, fs: TV1[0:1, (0 if nm == "kv" else D) + 512 * fs:
                                (0 if nm == "kv" else D) + 512 * fs + 512]),
        ])
        fence("c2")
        c2_bar = s.barrier(("act", "pe", "dve", "pool"))
        dump("r2", PHIQ[:], [128, CH * D])
        d2 = phase_D(1, XT, r2)

        fence("d2")
        rl_evs = [s.dma("sync", PHIQ[:], spq_d[:, :],
                        deps=c2_bar + s.barrier(("act",))),
                  s.dma("sync", PHIK[:], spk_d[:, :], deps=c2_bar),
                  s.dma("sync", KV[:], spkv_d[:, :], deps=c2_bar)]
        do3 = s.dma("sync", CARKV[0:1, :], spo_d[0:1, 0:D], deps=c2_bar)
        do4 = s.dma("sync", CARK[0:1, :], spo_d[0:1, D:2 * D], deps=c2_bar)
        cj1, cj2 = do3, do4

        fence("rl")
        dump("rlq", PHIQ[:], [128, CH * D])
        dump("rlkv", KV[:], [128, CH * D])
        dump("rlk", PHIK[:], [128, CH * D])
        dump("carkv1", CARKV[0:1, :], [1, D])
        r1 = phase_C(0, rl_evs, [cj1, cj2], [
            (FLAGOR[:], FLAGO[0:1, 0:1],
             lambda nm, fs: (CARKV if nm == "kv" else CARK)
             [0:1, 512 * fs:512 * fs + 512]),
        ])
        fence("c1")
        dump("r1", PHIQ[:], [128, CH * D])
        d1 = phase_D(0, XT, r1)
        car_last = s.barrier(("pe", "act"))
        XT_cm.__exit__(None, None, None)
        CARK_cm.__exit__(None, None, None)
        CARKV_cm.__exit__(None, None, None)
        TV1_cm.__exit__(None, None, None)
        BIASR_cm = nc.sbuf_tensor("BIASR", [1, D], F32)
        BIASR = BIASR_cm.__enter__()
        c_br = s.dma("sync", BIASR[:], bias_d[:, :], deps=car_last)

        fence("d1")
        e_bar = s.barrier(("dve", "act", "pe", "pool"))
        if "xmid" in DBG:
            dd = dbg_decl("xmid", [TOK, D])
            s.dma("sync", dd[:, :], xmid_d[:, :], deps=e_bar + d1 + d2)
        am_evs = []
        for mc in range(2 * CH):
            row0 = mc * 128
            ld = s.dma("sync", YB[:], xmid_d[row0:row0 + 128, :],
                       deps=yb_free[0] + e_bar + d1 + d2)
            iv = ln_stats(YB[:], M2[:, mc:mc + 1], I2[:, mc:mc + 1], [ld])
            ue = s.op("dve", lambda E, mc=mc: E.tensor_scalar(
                out=SL0, in0=YB[:], scalar1=M2[:, mc:mc + 1],
                scalar2=I2[:, mc:mc + 1], op0=OP.subtract, op1=OP.mult),
                deps=[iv], sig=True)
            yb_free[0] = [ue]
            am = s.op("dve", lambda E, mc=mc: E.tensor_reduce(
                out=AMX[:, mc:mc + 1], in_=SL0, axis=AX.X, op=OP.max,
                apply_absolute_value=True), deps=[ue], sig=True)
            am_evs.append(am)
        amx_ev = s.op("dve", lambda E: E.tensor_reduce(
            out=AMX[:, 0:1], in_=AMX[:, 0:8], axis=AX.X, op=OP.max,
            apply_absolute_value=True), deps=am_evs, sig=True, drain=True)
        cp1 = s.op("pool", lambda E: E.tensor_reduce(
            out=SC[0:1, 0:1], in_=AMX[:, 0:1], axis=AX.C, op=OP.max),
            deps=[amx_ev], sig=True, drain=True)
        cp2 = s.op("pool", lambda E: E.tensor_reduce(
            out=SC[0:1, 1:2], in_=AMW[:, 0:1], axis=AX.C, op=OP.max),
            deps=[amw_ev], sig=True, drain=True)
        cb = s.op("dve", lambda E: E.tensor_tensor(
            out=SC[0:1, 2:3], in0=SC[0:1, 0:1], in1=SC[0:1, 1:2], op=OP.max),
            deps=[cp1, cp2], sig=True, drain=True)
        bc1 = s.op("pool", lambda E: E.tensor_copy(
            CX[0:1, 0:8], SC[0:1, 0:1].broadcast_to((1, 8))), deps=[cp1],
            sig=True, drain=True)
        bc2 = s.op("pool", lambda E: E.tensor_copy(
            CX[0:1, 8:16], SC[0:1, 2:3].broadcast_to((1, 8))), deps=[cb],
            sig=True, drain=True)
        dci = s.dma("sync", cxi_d[:, :], CX[0:1, :], deps=[bc1, bc2])
        ccx = s.cc(lambda E: E.collective_compute(
            "AllReduce", OP.max, replica_groups=[list(range(NC))],
            ins=[cxi_d[:, :]], outs=[cxo_d[:, :]]), deps=[dci])
        dco = s.dma("sync", CX2[0:1, :], cxo_d[:, :], deps=[ccx])
        s1 = s.op("dve", lambda E: E.reciprocal(
            out=SC[0:1, 3:4], in_=CX2[0:1, 0:1]), deps=[dco], sig=True)
        s2 = s.op("dve", lambda E: E.tensor_scalar(
            out=SC[0:1, 4:5], in0=SC[0:1, 3:4], scalar1=K358, scalar2=None,
            op0=OP.mult), deps=[s1], sig=True, drain=True)
        s3 = s.op("dve", lambda E: E.reciprocal(
            out=SC[0:1, 5:6], in_=CX2[0:1, 8:9]), deps=[dco], sig=True)
        s4 = s.op("dve", lambda E: E.tensor_scalar(
            out=SC[0:1, 6:7], in0=SC[0:1, 5:6], scalar1=K358, scalar2=None,
            op0=OP.mult), deps=[s3], sig=True, drain=True)
        s5 = s.op("dve", lambda E: E.tensor_tensor(
            out=SC[0:1, 7:8], in0=CX2[0:1, 0:1], in1=CX2[0:1, 8:9],
            op=OP.mult), deps=[dco], sig=True)
        s6 = s.op("dve", lambda E: E.tensor_scalar(
            out=SC[0:1, 8:9], in0=SC[0:1, 7:8], scalar1=1.0 / (K358 * K358),
            scalar2=None, op0=OP.mult), deps=[s5], sig=True, drain=True)
        s7 = s.op("dve", lambda E: E.reciprocal(
            out=SC[0:1, 9:10], in_=SC[0:1, 8:9]), deps=[s6], sig=True,
            drain=True)
        bsc = s.op("act", lambda E: E.activation(
            out=BIASR[0:1, :], in_=BIASR[0:1, :], func=AF.Copy,
            scale=SC[0:1, 9:10]), deps=[s7, c_br], sig=True, drain=True)
        bevs = []
        for (col, src_ev, scol) in ((1, s2, 4), (2, s4, 6), (3, s6, 8)):
            mm = s.op("pe", lambda E, scol=scol: E.matmul(
                PB[5][:, 0:1], ONES1[:], SC[0:1, scol:scol + 1],
                start=True, stop=True),
                deps=[src_ev, c_o1] + bank_free[5], sig=True)
            cp = s.op("act", lambda E, col=col: E.copy(
                out=B128[:, col:col + 1], in_=PB[5][:, 0:1]), deps=[mm],
                sig=True, drain=True)
            bank_free[5] = [cp]
            bevs.append(cp)

        def emit_quant(src_ap, out_ap, scale_col, sA, sB, sC, sD, deps):
            e1 = s.op("act", lambda E: E.activation(
                out=sA, in_=src_ap, func=AF.Copy,
                scale=B128[:, scale_col:scale_col + 1]), deps=deps, sig=True,
                drain=True)
            e1b = s.op("act", lambda E: E.activation(
                out=sD, in_=sA, func=AF.Copy, scale=PHI4_64),
                deps=[e1], sig=True, drain=True)
            e2 = s.op("dve", lambda E: E.scalar_tensor_tensor(
                out=sC, in0=sA, scalar=PHI4_32, in1=sA,
                op0=OP.mult, op1=OP.subtract), deps=[e1], sig=True)
            e3 = s.op("dve", lambda E: E.scalar_tensor_tensor(
                out=sB, in0=sA, scalar=PHI4_32, in1=sC,
                op0=OP.mult, op1=OP.subtract), deps=[e2], sig=True)
            e5 = s.op("dve", lambda E: E.tensor_tensor(
                out=sC, in0=sD, in1=sA, op=OP.subtract), deps=[e1b, e3],
                sig=True)
            e6 = s.op("dve", lambda E: E.tensor_tensor(
                out=sD, in0=sD, in1=sC, op=OP.subtract), deps=[e5], sig=True)
            e7 = s.op("dve", lambda E: E.tensor_tensor(
                out=sD, in0=sD, in1=sB, op=OP.subtract), deps=[e6], sig=True)
            return s.op("dve", lambda E: E.scalar_tensor_tensor(
                out=out_ap, in0=sD, scalar=B128[:, 0:1], in1=sB,
                op0=OP.mult, op1=OP.add), deps=[e7, ind_ev], sig=True,
                drain=True)

        fence("e")
        dump("sc", SC[0:1, :], [1, 16])
        dump("cx2", CX2[0:1, :], [1, 16])
        dump("b128", B128[:, :], [128, 4])
        dump("m2", M2[:, :], [128, 2 * CH])
        dump("i2", I2[:, :], [128, 2 * CH])
        QX_cm = nc.sbuf_tensor("QX", [128, 16 * TOK], BF16)
        QX = QX_cm.__enter__()
        QW_cm = nc.sbuf_tensor("QW", [128, 4096], BF16)
        QW = QW_cm.__enter__()
        QW2_cm = nc.sbuf_tensor("QW2", [128, 4096], BF16)
        QW2 = QW2_cm.__enter__()
        QWS = [QW, QW2]
        qw_guard2 = [[], []]

        fence("scales")
        qx_evs = []
        for mc in range(2 * CH):
            row0 = mc * 128
            ld = s.dma("sync", YB[:], xmid_d[row0:row0 + 128, :],
                       deps=yb_free[0] + d1 + d2)
            ue = s.op("dve", lambda E, mc=mc: E.tensor_scalar(
                out=SL0, in0=YB[:], scalar1=M2[:, mc:mc + 1],
                scalar2=I2[:, mc:mc + 1], op0=OP.subtract, op1=OP.mult),
                deps=[ld] + qx_evs[-4:], sig=True, drain=True)
            ut_evs = transpose_to(
                lambda j: SL0[:, 128 * j:128 * j + 128],
                lambda jb: SL0[:, 512 * jb:512 * jb + 512],
                [ue])
            qlist = []
            for qq in range(4):
                srcq = SL0[:, 512 * qq:512 * qq + 512]
                sbase = YB if qq % 2 == 0 else SL1
                dst = bass.AP(QX, 4096 * qq + 128 * mc,
                              [[16 * TOK, 128], [TOK, 4], [1, 128]])
                e7 = emit_quant(srcq, dst, 1,
                                sbase[:, 0:512], sbase[:, 512:1024],
                                sbase[:, 1024:1536], sbase[:, 1536:2048],
                                [ut_evs[qq], bevs[0]] + qlist[-2:-1])
                qlist.append(e7)
            qx_evs += qlist[-2:]
            yb_free[0] = qlist[-2:]
        fence("qx")
        qx_done = s.barrier(("dve", "pool", "act"))
        dump("qx", QX[:], [128, 16 * TOK], BF16)

        prevq = [None, None]
        for os in range(4):
            fence(f"f{os}")
            fill_evs = [[] for _ in range(2 * CH)]
            for kcr in range(2):
                wv = (2 * os + kcr) % 2
                qw = QWS[wv]
                wb = WB[kcr]
                src = bass.AP(
                    wffn_d, os * 512 + kcr * 8 * 128 * D,
                    [[D, 128], [128 * D, 8], [1, 512]])
                dd = s.dma("sync",
                           wb[:].rearrange("p (k c) -> p k c", k=8), src,
                           deps=wb_free[kcr] + (qx_done if os == 0 else []))
                qdone = []
                for qq in range(8):
                    srcq = wb[:, 512 * qq:512 * qq + 512]
                    sbase = SL0 if qq % 2 == 0 else SL1
                    xdeps = [dd, bevs[1]] + qdone[-2:-1] + qx_done
                    if qq < 2:
                        xdeps += qw_guard2[wv]
                        if prevq[0] is not None:
                            xdeps += [prevq[0][6 + qq]]
                    w7 = emit_quant(
                        srcq, qw[:, 512 * qq:512 * qq + 512], 2,
                        sbase[:, 0:512], sbase[:, 512:1024],
                        sbase[:, 1024:1536], sbase[:, 1536:2048],
                        xdeps)
                    qdone.append(w7)
                prevq[0] = qdone
                if os == 0 and kcr == 0:
                    dump("qw00", qw[:], [128, 4096], BF16)
                mm_last = [None] * (2 * CH)
                for kc8 in range(8):
                    kc = kcr * 8 + kc8
                    for mc in range(2 * CH):
                        st = (kc == 0)
                        deps = [qdone[kc8]]
                        if st:
                            deps += bank_free[mc]
                        mm_last[mc] = s.op(
                            "pe", lambda E, mc=mc, kc=kc, kc8=kc8, qw=qw:
                            E.matmul(
                                PB[mc][:],
                                QX[:, TOK * kc + 128 * mc:
                                   TOK * kc + 128 * mc + 128],
                                qw[:, 512 * kc8:512 * kc8 + 512],
                                start=(kc == 0), stop=False),
                            deps=deps,
                            sig=((kc8 == 7 and mc == 2 * CH - 1)
                                 or kc == 15))
                wb_free[kcr] = [mm_last[2 * CH - 1]]
                for mc in range(2 * CH):
                    fill_evs[mc].append(mm_last[mc])
                qw_guard2[wv] = [mm_last[2 * CH - 1]]
                if os == 0 and kcr == 0:
                    dump("qw00b", qw[:], [128, 4096], BF16)
                    dump("qx7", QX[:, 7168:7168 + 1024], [128, 1024], BF16)
            for mc in range(2 * CH):
                bm = s.op("pe", lambda E, mc=mc, os=os: E.matmul(
                    PB[mc][:], ONES1[:],
                    BIASR[0:1, 512 * os:512 * os + 512],
                    start=False, stop=True),
                    deps=[bsc] + fill_evs[mc], sig=True)
                fill_evs[mc].append(bm)
            if os == 0 and "psum00" in DBG:
                pd = dbg_decl("psum00", [128, 512])
                cpev = s.op("act", lambda E: E.copy(
                    out=SL1[:, 0:512], in_=PB[0][:]),
                    deps=fill_evs[0][-1:], sig=True)
                s.dma("sync", pd[:, :], SL1[:, 0:512], deps=[cpev])
                s.gate = s.barrier(("pe", "dve", "act", "pool")) +                     [("dsync", s.cnt["dsync"])]
            for mc in range(2 * CH):
                row0 = mc * 128
                xld = s.dma("sync", YB[:, 512 * os:512 * os + 512],
                            xmid_d[row0:row0 + 128, 512 * os:512 * os + 512],
                            deps=yb_free[0] + d1 + d2)
                ev = s.op("dve", lambda E, mc=mc, os=os:
                          E.scalar_tensor_tensor(
                              out=YB[:, 512 * os:512 * os + 512],
                              in0=PB[mc][:], scalar=B128[:, 3:4],
                              in1=YB[:, 512 * os:512 * os + 512],
                              op0=OP.mult, op1=OP.add),
                          deps=[xld, bevs[2]] + fill_evs[mc][-1:], sig=True,
                          drain=True)
                bank_free[mc] = [ev]
                if os == 0 and mc == 0:
                    dump("ffn00", YB[:, 0:512], [128, 512])
                sto = s.dma("sync",
                            y_d[row0:row0 + 128, 512 * os:512 * os + 512],
                            YB[:, 512 * os:512 * os + 512], deps=[ev])
                yb_free[0] = [sto]

        QW2_cm.__exit__(None, None, None)
        QW_cm.__exit__(None, None, None)
        QX_cm.__exit__(None, None, None)
        BIASR_cm.__exit__(None, None, None)

        last_out = ("dpool", s.cnt["dpool"])
        s.op("pool", lambda E: E.engine_nop(), deps=[last_out])

        with nc.Block() as block:
            s.emit(block, semh)
    return nc


_NC_CACHE = {}


def kernel(L0_attn_x, L0_attn_W_qkv, L0_attn_W_out, L0_ffn_weight,
           L0_ffn_bias):
    x = np.ascontiguousarray(
        np.asarray(L0_attn_x, np.float32)).reshape(NC * TOK, D)
    wqkvT = np.ascontiguousarray(np.asarray(L0_attn_W_qkv, np.float32).T)
    woutT = np.ascontiguousarray(np.asarray(L0_attn_W_out, np.float32).T)
    wffnT = np.ascontiguousarray(np.asarray(L0_ffn_weight, np.float32).T)
    biasr = np.ascontiguousarray(
        np.asarray(L0_ffn_bias, np.float32)).reshape(1, D)
    tri = np.triu(np.ones((128, 128), np.float32))
    ident = np.eye(128, dtype=np.float32)
    ones1 = np.ones((1, 128), np.float32)
    onesc = np.ones((128, 1), np.float32)

    if "nc" not in _NC_CACHE:
        _NC_CACHE["nc"] = build_program()
    nc = _NC_CACHE["nc"]
    in_maps = []
    for c in range(NC):
        in_maps.append({
            "x": x[c * TOK:(c + 1) * TOK],
            "wqkvT": wqkvT, "woutT": woutT, "wffnT": wffnT, "biasr": biasr,
            "tri": tri, "ident": ident, "ones1": ones1, "onesc": onesc,
            "flago": np.full((1, 1), float(c % 2), np.float32),
            "flagor": np.full((1, 128), float(c % 2), np.float32),
            "flage": np.full((1, 1), float(1 - c % 2), np.float32),
        })
    res = run_bass_kernel_spmd(nc, in_maps, core_ids=list(range(NC)))
    y = np.concatenate([np.asarray(r["y"]) for r in res.results], axis=0)
    return y.reshape(4, 2048, 2048)
